# version 07
# baseline (speedup 1.0000x reference)
import contextlib
import numpy as np
import ml_dtypes
import concourse.bass as bass
import concourse.mybir as mybir

F32 = mybir.dt.float32
BF16 = mybir.dt.bfloat16
ALU = mybir.AluOpType
AF = mybir.ActivationFunctionType
AX = mybir.AxisListType
bfd = ml_dtypes.bfloat16

T_OWN = 2048
L = 4096
D = 1024
H = 16
HS = 64
M = 64
NKC = 8
NE = 8
EPS = 1e-5
CLAMP = 10000.0
REPS = 1e-8


def ecol(h):
    if h < 7:
        return 65 * h
    if h < 14:
        return 512 + 65 * (h - 7)
    return 1024 + 65 * (h - 14)


class Sem:
    def __init__(self, ctx, nc, name):
        self.h = ctx.enter_context(nc.semaphore(name))
        self.n = 0


class KB:
    def __init__(self, upto=99, dbg=()):
        self.upto = upto
        self.dbg = set(dbg)
        self.nc = bass.Bass('TRN2', target_bir_lowering=False,
                            detect_race_conditions=False)
        self.ctx = contextlib.ExitStack()
        self.ops = {k: [] for k in ("sync", "gpsimd", "tensor", "vector",
                                    "scalar")}
        self.dumps = []

    def _op(self, eng, build, waits, inc, k):
        w = tuple((s.h, int(v)) for (s, v) in waits)
        ih = inc.h if inc is not None else None
        def f(e):
            for hh, vv in w:
                e.wait_ge(hh, vv)
            ins = build(e)
            if ih is not None:
                ins.then_inc(ih, k)
        self.ops[eng].append(f)
        if inc is not None:
            inc.n += k

    def t(self, build, waits=(), inc=None, k=1):
        self._op('tensor', build, waits, inc, k)

    def v(self, build, waits=(), inc=None, k=1):
        self._op('vector', build, waits, inc, k)

    def s(self, build, waits=(), inc=None, k=1):
        self._op('scalar', build, waits, inc, k)

    def g(self, build, waits=(), inc=None, k=16):
        self._op('gpsimd', build, waits, inc, k)

    def y(self, build, waits=(), inc=None, k=16):
        self._op('sync', build, waits, inc, k)

    def vsync(self):
        self.v(lambda e: e.tensor_copy(self.vsync_t[:, 0:4],
                                       self.vsync_t[:, 4:8]))
        self.v(lambda e: e.tensor_copy(self.vsync_t[:, 0:4],
                                       self.vsync_t[:, 4:8]), inc=self.sVE)
        self.wait('vector', self.sVE)

    def wait(self, eng, sem, val=None):
        v = sem.n if val is None else val
        hh = sem.h
        self.ops[eng].append(lambda e: e.wait_ge(hh, int(v)))

    def build(self):
        nc = self.nc
        ctx = self.ctx
        sb = lambda name, shape, dtp, **kw: ctx.enter_context(
            nc.sbuf_tensor(name, shape, dtp, **kw))
        LEFT = dict(side="left"); RIGHT = dict(side="right")
        self.sbf = sb

        dt = lambda n, s: nc.dram_tensor(n, s, F32, kind="ExternalInput").ap()
        self.x_own = dt('x_own', [T_OWN, D])
        self.x_part = dt('x_part', [T_OWN, D])
        self.wq_d = dt('wq', [D, D]); self.wk_d = dt('wk', [D, D])
        self.wv_d = dt('wv', [D, D]); self.wo_d = dt('wo', [D, D])
        self.bo_d = dt('bo', [D])
        self.wkT_d = dt('wkT16', [H, HS, D])
        self.routerT_d = dt('routerT', [128, NKC * NE])
        self.gamma_d = dt('gamma', [D]); self.beta_d = dt('beta', [D])
        self.WG_d = dt('WG', [NE, 16, 128, 2048])
        self.DN_d = dt('DN', [NE, 16, 128, D])
        self.out_d = nc.dram_tensor('out', [T_OWN, D], F32,
                                    kind="ExternalOutput").ap()

        id128f_np = np.eye(128, dtype=np.float32)
        id128b_np = np.eye(128, dtype=bfd)
        ones128_np = np.ones((128, 128), dtype=bfd)
        e16_np = np.zeros((64, 256), dtype=bfd)
        for h in range(16):
            e16_np[:, 16 * h + h] = 1.0
        b16_np = np.zeros((16, 1024), dtype=bfd)
        for p in range(8):
            b16_np[2 * p, 128 * p: 128 * p + 64] = 1.0
            b16_np[2 * p + 1, 128 * p + 64: 128 * p + 128] = 1.0
        def ipack(c):
            t = np.zeros((64, H * 64), dtype=bfd)
            for h in range(H):
                t[:, 64 * h: 64 * h + 64] = c * np.eye(64, dtype=bfd)
            return t
        inl = lambda a, n: nc.inline_tensor(a, n)
        self.d_id128f = inl(id128f_np, 'id128f')
        self.d_id128b = inl(id128b_np, 'id128b')
        self.d_ones = inl(ones128_np, 'ones128')
        self.d_e16 = inl(e16_np, 'e16c')
        self.d_b16 = inl(b16_np, 'b16c')
        self.d_i7 = inl(ipack(7.0), 'i7p')
        self.d_i15 = inl(ipack(15.0), 'i15p')
        self.d_i13 = inl(ipack(13.0), 'i13p')
        self.d_onesrow = inl(np.ones((1, 128), dtype=bfd), 'onesrow')
        self.d_onesf = inl(np.ones((128, 128), np.float32), 'onesf')
        d_epsc = inl((np.full((128, 2), EPS, np.float32) *
                      np.array([1.0, -1.0], np.float32)), 'epsc')
        cp = np.zeros((128, 512), np.float32)
        cp[:, 0:128] = CLAMP; cp[:, 128:256] = -CLAMP
        cp[:, 256:384] = 1.0; cp[:, 384:512] = REPS
        d_clipp = inl(cp, 'clipp')

        at = lambda name, shape, dtp, off: nc.alloc_sbuf_tensor_at(
            name, shape, dtp, offset=off)
        self.at = at
        B0 = 17504
        assert B0 >= nc.sbuf_base + 1000
        self.resid = at('resid', [128, NKC * T_OWN], F32, B0)
        self.idf = at('idf', [128, 128], F32, B0 + 65536)
        self.idb = at('idb', [128, 128], BF16, B0 + 66048)
        self.onesb = at('onesb', [128, 128], BF16, B0 + 66304)
        self.onesrow = at('onesrow_s', [1, 128], BF16, B0 + 66560)
        self.bo_sb = at('bo_sb', [128, 8], F32, B0 + 66816)
        self.gam = at('gam', [128, 8], F32, B0 + 66848)
        self.bet = at('bet', [128, 8], F32, B0 + 66880)
        self.vsync_t = at('vsync_t', [128, 8], F32, B0 + 66912)
        self.epsc = at('epsc', [128, 2], F32, B0 + 66944)
        self.clipp = at('clipp', [128, 512], F32, B0 + 67072)
        self.C0 = B0 + 69120
        self.xn1_own = at('xn1_own', [128, NKC * T_OWN], BF16, self.C0)
        self.D0 = 17504 + 2048 + 99840
        self.xn1_part = at('xn1_part', [128, NKC * T_OWN], BF16, self.D0)
        self.E0 = 17504 + 2048 + 132608
        self.F0 = 17504 + 2048 + 165376
        self.G0 = 17504 + 2048 + 181760
        S = 17504 + 2048 + 198144
        self.kl = at('kl', [64, H * M], BF16, S); S += 2048
        self.Wh = at('Wh', [64, H * HS], BF16, S); S += 2048
        self.xmean8 = at('xmean8', [128, NKC * M], F32, S); S += 2048
        self.xmeanb = at('xmeanb', [128, NKC * M], BF16, S); S += 1024
        self.rec16 = at('rec16', [64, 16], F32, S); S += 64
        self.st16 = at('st16', [64, 16], F32, S); S += 64
        self.st2 = at('st2', [64, 16], F32, S); S += 64
        self.recv = at('recv', [64, 16], F32, S); S += 64
        self.colr = at('colr', [16, 4], F32, S); S += 32
        self.scrow = at('scrow', [1, 16], BF16, S); S += 32
        self.scol = at('scol', [64, 16], F32, S); S += 64
        assert S <= 229344, ('SBUF overflow', S)

        self.pb = [ctx.enter_context(nc.psum_tensor(f'pb{i}', [128, 512],
                                                    F32)) for i in range(8)]
        self.pbb = [t.bitcast(BF16) for t in self.pb]

        SS = lambda n: Sem(ctx, nc, n)
        self.sX = SS('sX'); self.sW = SS('sW')
        self.sPE = SS('sPE'); self.sVE = SS('sVE'); self.sSC = SS('sSC')
        self.sST = SS('sST')
        self.sGU = SS('sGU'); self.sDN = SS('sDN'); self.sGP = SS('sGP')

        for (dst, srcd) in [(self.idf, self.d_id128f),
                            (self.idb, self.d_id128b),
                            (self.onesb, self.d_ones),
                            (self.onesrow, self.d_onesrow)]:
            self.y(lambda e, dst=dst, srcd=srcd: e.dma_start(
                dst[:, :], srcd.ap()[:, :]), inc=self.sX)
        self.y(lambda e: e.dma_start(
            self.bo_sb[:, :], self.bo_d.rearrange('(c p) -> p c', p=128),
            allow_slow_non_contiguous=True),
            inc=self.sX)
        self.y(lambda e: e.dma_start(
            self.gam[:, :], self.gamma_d.rearrange('(c p) -> p c', p=128),
            allow_slow_non_contiguous=True),
            inc=self.sX)
        self.y(lambda e: e.dma_start(
            self.bet[:, :], self.beta_d.rearrange('(c p) -> p c', p=128),
            allow_slow_non_contiguous=True),
            inc=self.sX)
        self.y(lambda e, d_epsc=d_epsc: e.dma_start(self.epsc[:, :], d_epsc.ap()[:, :]),
               inc=self.sX)
        self.y(lambda e, d_clipp=d_clipp: e.dma_start(
            self.clipp[:, :], d_clipp.ap()[:, :]), inc=self.sX)
        self.onesf = self.at('onesf_sb', [128, 128], F32,
                             self.E0 + 26624)
        self.y(lambda e: e.dma_start(self.onesf[:, :],
                                     self.d_onesf.ap()[:, :]), inc=self.sX)
        self.const_ready = self.sX.n
        self.v(lambda e: e.memset(self.vsync_t[:, :], 0.0))

        phases = [self.ph1, self.ph2, self.ph3, self.ph4, self.ph5, self.ph6,
                  self.ph7]
        for i, ph in enumerate(phases):
            if self.upto >= i + 1:
                ph(sb)

        self.finish()
        return self.nc

    def dump(self, name, ap_sb, shape, dtp=F32):
        if name in self.dbg:
            dd = self.nc.dram_tensor('dbg_' + name, list(shape), dtp,
                                     kind="ExternalOutput").ap()
            self.dumps.append((dd, ap_sb))

    def dump_now(self, name, ap_sb, shape, dtp=F32):
        if name in self.dbg:
            dd = self.nc.dram_tensor('dbg_' + name, list(shape), dtp,
                                     kind="ExternalOutput").ap()
            self.y(lambda e, dd=dd, ap_sb=ap_sb: e.dma_start(dd, ap_sb),
                   waits=[(self.sVE, self.sVE.n), (self.sSC, self.sSC.n),
                          (self.sPE, self.sPE.n)],
                   inc=self.sST)
            for eng in ('vector', 'scalar', 'tensor', 'gpsimd'):
                self.wait(eng, self.sST)

    def finish(self):
        for dd, src in self.dumps:
            self.y(lambda e, dd=dd, src=src: e.dma_start(dd, src),
                   waits=[(self.sVE, self.sVE.n), (self.sSC, self.sSC.n),
                          (self.sPE, self.sPE.n)],
                   inc=self.sST)
        self.wait('sync', self.sST)
        with self.nc.Block() as block:
            for eng in ("sync", "gpsimd", "tensor", "vector", "scalar"):
                fns = self.ops[eng]
                getattr(block, eng)(lambda e, fns=fns: [f(e) for f in fns])

    def ph1(self, sb):
        sX, sPE, sVE, sSC = self.sX, self.sPE, self.sVE, self.sSC
        pb = self.pb
        E0 = self.E0; at = self.at
        xtm = [at(f'xtm{i}', [128, D], F32, E0 + 4096 * i) for i in range(2)]
        xfh = at('xfh', [128, NKC * 512], BF16, E0 + 8192)
        sqb = [at(f'sqb{i}', [128, 512], BF16, E0 + 16384 + 1024 * i)
               for i in range(2)]
        mb = at('mb', [128, 512], F32, E0 + 18432)
        rb = at('rb', [128, 512], F32, E0 + 20480)
        tmp = at('tmp_p1', [128, 512], F32, E0 + 22528)
        tmp2 = at('tmp2_p1', [128, 512], F32, E0 + 24576)

        chunk_free = [(None, 0), (None, 0)]
        bank_free = [(None, 0), (None, 0)]
        stats_free = 0
        xfh_free_pe = 0
        xfh_free_ve = 0
        xfh_free_sc = 0
        for slab in range(8):
            own = slab < 4
            src = self.x_own if own else self.x_part
            t0 = (slab % 4) * 512
            first_v = True
            first_s = True
            for cch in range(4):
                buf = xtm[cch % 2]
                cf = chunk_free[cch % 2]
                self.y(lambda e, buf=buf, src=src, t0=t0, cch=cch:
                       e.dma_start(buf[:, :],
                                   src[t0 + 128 * cch:t0 + 128 * (cch + 1),
                                       :]),
                       waits=([cf] if cf[0] is not None else []), inc=sX)
                ld = sX.n
                for kc in range(NKC):
                    bank = pb[2 + (kc % 2)]
                    bf_ = bank_free[kc % 2]
                    w = [bf_] if bf_[0] is not None else []
                    if kc == 0:
                        w.append((sX, ld))
                        if slab == 0 and cch == 0:
                            w.append((sX, self.const_ready))
                    self.t(lambda e, bank=bank, buf=buf, kc=kc:
                           e.transpose(bank[:, 0:128],
                                       buf[:, 128 * kc:128 * (kc + 1)],
                                       self.idf[:, :]),
                           waits=w, inc=sPE)
                    wv = [(sPE, sPE.n)]
                    on_v = (kc % 2 == 0)
                    if not own:
                        if on_v and first_v:
                            wv += [(sPE, xfh_free_pe), (sSC, xfh_free_sc)]
                            first_v = False
                        elif not on_v and first_s:
                            wv += [(sPE, xfh_free_pe), (sVE, xfh_free_ve)]
                            first_s = False
                    if own:
                        dcol = kc * T_OWN + t0 + 128 * cch
                        if on_v:
                            self.v(lambda e, bank=bank, dcol=dcol:
                                   e.tensor_copy(
                                       self.resid[:, dcol:dcol + 128],
                                       bank[:, 0:128]),
                                   waits=wv, inc=sVE)
                        else:
                            self.s(lambda e, bank=bank, dcol=dcol: e.copy(
                                self.resid[:, dcol:dcol + 128],
                                bank[:, 0:128]),
                                waits=wv, inc=sSC)
                    else:
                        xsl = xfh[:, 512 * kc + 128 * cch:
                                  512 * kc + 128 * cch + 128]
                        if on_v:
                            self.v(lambda e, bank=bank, xsl=xsl:
                                   e.tensor_copy(xsl, bank[:, 0:128]),
                                   waits=wv, inc=sVE)
                        else:
                            self.s(lambda e, bank=bank, xsl=xsl: e.copy(
                                xsl, bank[:, 0:128]),
                                waits=wv, inc=sSC)
                    bank_free[kc % 2] = (sVE, sVE.n) if on_v else \
                        (sSC, sSC.n)
                chunk_free[cch % 2] = bank_free[1]
            ve_done, sc_done = sVE.n, sSC.n
            sq_prod = []
            for kc in range(NKC):
                q = sqb[kc % 2]
                if own:
                    hsrc = self.resid[:, kc * T_OWN + t0:
                                      kc * T_OWN + t0 + 512]
                else:
                    hsrc = xfh[:, 512 * kc:512 * (kc + 1)]
                w = [(sVE, ve_done), (sSC, sc_done)] if kc == 0 else []
                if kc >= 2:
                    w.append((sPE, sq_prod[kc - 2]))
                self.s(lambda e, q=q, hsrc=hsrc: e.activation(
                    q[:, :], hsrc, AF.Square),
                    waits=w, inc=sSC)
                sq_sc = sSC.n
                self.t(lambda e, kc=kc, hsrc=hsrc, own=own: e.matmul(
                    pb[0][:, :],
                    (self.onesf if own else self.onesb)[:, :], hsrc,
                    start=(kc == 0), stop=(kc == NKC - 1)),
                    waits=([(sVE, ve_done), (sSC, sc_done)] +
                           ([(sVE, stats_free)] if kc == 0 else []))
                    if kc == 0 else [])
                self.t(lambda e, q=q, kc=kc: e.matmul(
                    pb[1][:, :], self.onesb[:, :], q[:, :],
                    start=(kc == 0), stop=(kc == NKC - 1)),
                    waits=[(sSC, sq_sc)], inc=sPE)
                sq_prod.append(sPE.n)
            xfh_free_pe = sPE.n
            self.v(lambda e: e.tensor_scalar_mul(mb[:, :], pb[0][:, :],
                                                 1.0 / D),
                   waits=[(sPE, sPE.n)])
            self.v(lambda e: e.tensor_mul(tmp[:, :], mb[:, :], mb[:, :]),
                   inc=sVE)
            self.s(lambda e: e.add(tmp[:, :], tmp[:, :], self.epsc[:, 1:2]),
                   waits=[(sVE, sVE.n)], inc=sSC)
            self.v(lambda e: e.scalar_tensor_tensor(
                rb[:, :], pb[1][:, :], 1.0 / D, tmp[:, :],
                ALU.mult, ALU.subtract), waits=[(sSC, sSC.n)],
                inc=sVE)
            stats_free = sVE.n
            self.s(lambda e: e.activation(tmp[:, :], rb[:, :], AF.Ln),
                   waits=[(sVE, sVE.n)])
            self.s(lambda e: e.activation(rb[:, :], tmp[:, :], AF.Exp,
                                          scale=-0.5), inc=sSC)
            stat_done = sSC.n
            dstx = self.xn1_own if own else self.xn1_part
            for kc in range(NKC):
                if own:
                    xin = self.resid[:, kc * T_OWN + t0: kc * T_OWN + t0 + 512]
                else:
                    xin = xfh[:, 512 * kc:512 * (kc + 1)]
                dcol = kc * T_OWN + t0
                self.v(lambda e, xin=xin: e.tensor_sub(tmp2[:, :], xin,
                                                       mb[:, :]),
                       waits=[(sSC, stat_done)])
                self.v(lambda e, dstx=dstx, dcol=dcol: e.tensor_mul(
                    dstx[:, dcol:dcol + 512], tmp2[:, :], rb[:, :]))
                self.v(lambda e, dstx=dstx, dcol=dcol, kc=kc, slab=slab:
                       e.tensor_reduce(
                           self.xmean8[:, kc * M + 8 * slab:
                                       kc * M + 8 * slab + 8],
                           dstx[:, dcol:dcol + 512].rearrange(
                               'p (g k) -> p g k', g=8),
                           AX.X, ALU.add), inc=sVE)
            if not own:
                xfh_free_sc = sSC.n
            xfh_free_ve = sVE.n if not own else xfh_free_ve
        self.vsync()
        sv_xm = sVE.n
        for kc in range(NKC):
            self.s(lambda e, kc=kc: e.activation(
                self.xmeanb[:, kc * M:(kc + 1) * M],
                self.xmean8[:, kc * M:(kc + 1) * M], AF.Copy,
                scale=1.0 / 64),
                waits=([(sVE, sv_xm)] if kc == 0 else []),
                inc=(sSC if kc == NKC - 1 else None))
        self.wait('vector', sSC)
        self.v(lambda e: e.tensor_copy(self.vsync_t[:, 0:4],
                                       self.vsync_t[:, 4:8]), inc=sVE)
        self.p1_done_v = sVE.n
        self.dump('xn1_own', self.xn1_own[:, :], [128, NKC * T_OWN], BF16)
        self.dump('xn1_part', self.xn1_part[:, :], [128, NKC * T_OWN], BF16)
        self.dump('xmean', self.xmeanb[:, :], [128, NKC * M], BF16)
        self.dump('resid1', self.resid[:, :], [128, NKC * T_OWN], F32)

    def ph2(self, sb):
        sX, sW, sPE, sVE, sSC = self.sX, self.sW, self.sPE, self.sVE, self.sSC
        pb = self.pb
        E0 = self.E0; at = self.at
        wq = at('wq_sb', [128, NKC * D], BF16, self.F0)
        wkh = at('wk_half', [128, 4 * D], BF16, E0 + 0)
        wkt = [at(f'wkt{i}', [64, D], BF16, E0 + 8192 + 2048 * i)
               for i in range(2)]
        self.i7 = at('i7', [64, 1024], BF16, E0 + 12288)
        self.i15 = at('i15', [64, 1024], BF16, E0 + 14336)
        self.i13 = at('i13', [64, 1024], BF16, E0 + 16384)
        self.zTA = at('zTA', [64, H * M], F32, E0 + 0)
        self.s2n = at('s2n', [64, H * M], F32, E0 + 4096)
        self.zA = at('zA', [64, H * M], F32, E0 + 4096)
        self.xs1 = at('xs1', [64, H * M], F32, E0 + 18432)
        self.xs2 = at('xs2', [64, H * M], F32, E0 + 22528)
        self.s2nT = at('s2nT', [64, H * M], F32, E0 + 26624)
        self.ql = at('ql', [64, H * M], BF16, E0 + 30720)
        self.wq = wq
        st2 = self.st2; colr = self.colr; scrow = self.scrow
        scol = self.scol
        for (dst, srcd) in [(self.i7, self.d_i7), (self.i15, self.d_i15),
                            (self.i13, self.d_i13)]:
            self.y(lambda e, dst=dst, srcd=srcd: e.dma_start(
                dst[:, :], srcd.ap()[:, :]),
                waits=[(sVE, self.p1_done_v)], inc=sX)
        ipack_ready = sX.n
        for kc in range(NKC):
            self.g(lambda e, kc=kc: e.dma_start(
                wq[:, D * kc:D * (kc + 1)],
                self.wq_d[128 * kc:128 * (kc + 1), :]), inc=sW)
        w_ready = sW.n
        self.wq_ready = w_ready
        vstart = self.p1_done_v

        for h in range(H):
            bank = pb[4 + h // 8]
            off = 64 * (h % 8)
            for kc in range(NKC):
                w = []
                if h == 0 and kc == 0:
                    w = [(sW, w_ready), (sVE, vstart)]
                self.t(lambda e, h=h, kc=kc, bank=bank, off=off:
                       e.matmul(bank[:64, off:off + 64],
                                wq[:, D * kc + 64 * h:
                                   D * kc + 64 * h + 64],
                                self.xmeanb[:, kc * M:(kc + 1) * M],
                                start=(kc == 0),
                                stop=(kc == NKC - 1)),
                       waits=w, inc=(sPE if kc == NKC - 1 else None))
        for bi in range(2):
            self.s(lambda e, bi=bi: e.activation(
                self.ql[:, 512 * bi:512 * (bi + 1)], pb[4 + bi][:64, :],
                AF.Copy, scale=0.125),
                waits=[(sPE, sPE.n)], inc=sSC)
        ql_evac = sSC.n
        klh = [self.xs1, self.xs2]
        for half in range(2):
            self.g(lambda e, half=half: e.dma_start(
                wkh[:, :].rearrange('p (c f) -> p c f', c=4),
                self.wk_d[512 * half:512 * (half + 1), :].rearrange(
                    '(c p) f -> p c f', p=128)),
                waits=[(sPE, sPE.n)], inc=sW)
            wk_ready = sW.n
            for h in range(H):
                bank = pb[4 + h // 8]
                off = 64 * (h % 8)
                for kc4 in range(4):
                    kc = 4 * half + kc4
                    w = []
                    if kc4 == 0 and h == 0:
                        w = [(sW, wk_ready)]
                        w.append((sSC, sSC.n))
                    self.t(lambda e, h=h, kc=kc, kc4=kc4, bank=bank, off=off:
                           e.matmul(bank[:64, off:off + 64],
                                    wkh[:, D * kc4 + 64 * h:
                                        D * kc4 + 64 * h + 64],
                                    self.xmeanb[:, kc * M:(kc + 1) * M],
                                    start=(kc4 == 0), stop=(kc4 == 3)),
                           waits=w, inc=(sPE if (kc4 == 3 and h == H - 1)
                                         else None))
            for bi in range(2):
                if half == 0:
                    self.s(lambda e, bi=bi, half=half: e.activation(
                        klh[bi][:, :].rearrange('p (a) -> p a') if False else
                        klh[bi][:, 0:512], pb[4 + bi][:64, :],
                        AF.Copy, scale=1.0),
                        waits=[(sPE, sPE.n)], inc=sSC)
                else:
                    self.v(lambda e, bi=bi: e.tensor_add(
                        self.kl[:, 512 * bi:512 * (bi + 1)],
                        klh[bi][:, 0:512], pb[4 + bi][:64, :]),
                        waits=[(sPE, sPE.n)], inc=sVE)
        self.v(lambda e: e.engine_nop(), inc=sVE)
        qlkl_v = sVE.n
        qlkl = sSC.n

        for h in range(H):
            bank = pb[4 + h // 8]; off = 64 * (h % 8)
            self.t(lambda e, h=h, bank=bank, off=off: e.matmul(
                bank[:64, off:off + 64],
                self.ql[:, 64 * h:64 * h + 64],
                self.kl[:, 64 * h:64 * h + 64], start=True, stop=True),
                waits=([(sSC, qlkl), (sVE, qlkl_v)] if h == 0 else []),
                inc=(sPE if h == H - 1 else None))
        for bi in range(2):
            self.s(lambda e, bi=bi: e.activation(
                self.xs1[:, 512 * bi:512 * (bi + 1)], pb[4 + bi][:64, :],
                AF.Exp), waits=[(sPE, sPE.n)], inc=sSC)
        self.dump_now('xs1D', self.xs1[:, :], [64, H * M], F32)
        self.v(lambda e: e.tensor_reduce(
            self.st16[:, :], self.xs1[:, :].rearrange('p (h m) -> p h m', h=H),
            AX.X, ALU.add), waits=[(sSC, sSC.n)])
        self.vsync()
        self.v(lambda e: e.reciprocal(self.rec16[:, :], self.st16[:, :]),
               inc=sVE)
        self.vsync()
        sv_rec = self.sVE.n
        for h in range(H):
            self.s(lambda e, h=h: e.mul(
                self.s2n[:, 64 * h:64 * h + 64],
                self.xs1[:, 64 * h:64 * h + 64],
                self.rec16[:, h:h + 1]),
                waits=([(self.sVE, sv_rec)] if h == 0 else []),
                inc=(self.sSC if h == H - 1 else None))
        self.wait('vector', self.sSC)
        self.dump_now('s2nD', self.s2n[:, :], [64, H * M], F32)
        self.dump_now('rec16D', self.rec16[:, :], [64, 16], F32)
        self.v(lambda e: e.tensor_mul(self.st16[:, :], self.st16[:, :],
                                      self.rec16[:, :]), inc=sVE)
        s2n_done = sVE.n
        for h in range(H):
            bank = pb[4 + h // 8]; off = 64 * (h % 8)
            self.t(lambda e, h=h, bank=bank, off=off: e.matmul(
                bank[:64, off:off + 64], self.s2n[:, 64 * h:64 * h + 64],
                self.idf[:64, :64], start=True, stop=True),
                waits=([(sVE, s2n_done)] if h == 0 else []),
                inc=(sPE if h == H - 1 else None))
        for bi in range(2):
            self.v(lambda e, bi=bi: e.tensor_copy(
                self.s2nT[:, 512 * bi:512 * (bi + 1)], pb[4 + bi][:64, :]),
                waits=[(sPE, sPE.n)], inc=sVE)
        s2nT_done = sVE.n
        self.dump_now('s2nD2', self.s2n[:, :], [64, H * M], F32)

        st2 = self.st2
        self.v(lambda e: e.tensor_reduce(
            st2[:, :], self.s2nT[:, :].rearrange('p (h m) -> p h m', h=H),
            AX.X, ALU.add), waits=[(sVE, s2nT_done)], inc=sVE)
        self.t(lambda e: e.matmul(pb[6][:16, 0:64], self.st16[:, :],
                                  self.idf[:64, :64], start=True, stop=True),
               waits=[(sVE, sVE.n)])
        self.t(lambda e: e.matmul(pb[6][:16, 64:128], st2[:, :],
                                  self.idf[:64, :64], start=True, stop=True),
               inc=sPE)

        self.v(lambda e: e.tensor_reduce(colr[:, 0:1], pb[6][:16, 0:64],
                                         AX.X, ALU.max),
               waits=[(sPE, sPE.n)])
        self.v(lambda e: e.tensor_reduce(colr[:, 1:2], pb[6][:16, 64:128],
                                         AX.X, ALU.max))
        self.vsync()
        self.v(lambda e: e.tensor_mul(colr[:, 2:3], colr[:, 0:1],
                                      colr[:, 1:2]))
        self.vsync()
        self.v(lambda e: e.reciprocal(colr[:, 3:4], colr[:, 2:3]), inc=sVE)
        self.t(lambda e: e.transpose(pb[6][:1, 0:16], colr[:, 3:4],
                                     self.idf[:16, :16]),
               waits=[(sVE, sVE.n)], inc=sPE)
        self.v(lambda e: e.tensor_copy(scrow[:, :], pb[6][:1, 0:16]),
               waits=[(sPE, sPE.n)], inc=sVE)
        self.t(lambda e: e.matmul(pb[6][:64, 32:48], self.onesrow[:1, 0:64],
                                  scrow[:1, :], start=True, stop=True),
               waits=[(sVE, sVE.n)], inc=sPE)
        self.v(lambda e: e.tensor_copy(scol[:, :], pb[6][:64, 32:48]),
               waits=[(sPE, sPE.n)], inc=sVE)
        self.vsync()
        self.dump_now('scolD', scol[:, :], [64, 16], F32)
        self.dump_now('s2nTD', self.s2nT[:, :], [64, H * M], F32)
        self.dump_now('st16D', self.st16[:, :], [64, 16], F32)
        self.dump_now('colrD', colr[:, :], [16, 4], F32)
        sv_sc = sVE.n
        for h in range(H):
            self.s(lambda e, h=h: e.mul(
                self.zA[:, 64 * h:64 * h + 64],
                self.s2nT[:, 64 * h:64 * h + 64], scol[:, h:h + 1]),
                waits=([(sVE, sv_sc)] if h == 0 else []),
                inc=(sSC if h == H - 1 else None))
        self.wait('vector', sSC)
        self.v(lambda e: e.tensor_copy(self.vsync_t[:, 0:4],
                                       self.vsync_t[:, 4:8]), inc=sVE)
        self.dump_now('z0D', self.zA[:, :], [64, H * M], F32)
        for h in range(H):
            bank = pb[4 + h // 8]; off = 64 * (h % 8)
            self.t(lambda e, h=h, bank=bank, off=off: e.matmul(
                bank[:64, off:off + 64], self.zA[:, 64 * h:64 * h + 64],
                self.idf[:64, :64], start=True, stop=True),
                waits=([(sVE, sVE.n)] if h == 0 else []),
                inc=(sPE if h == H - 1 else None))
        for bi in range(2):
            self.v(lambda e, bi=bi: e.tensor_copy(
                self.zTA[:, 512 * bi:512 * (bi + 1)], pb[4 + bi][:64, :]),
                waits=[(sPE, sPE.n)], inc=sVE)

        Z, ZT = self.zA, self.zTA
        for it in range(6):
            for h in range(H):
                b1 = pb[4 + h // 8]; b2 = pb[6 + h // 8]
                off = 64 * (h % 8)
                self.t(lambda e, h=h, b1=b1, off=off, Z=Z: e.matmul(
                    b1[:64, off:off + 64], self.s2nT[:, 64 * h:64 * h + 64],
                    Z[:, 64 * h:64 * h + 64], start=True, stop=True),
                    waits=([(sVE, sVE.n)] if h == 0 else []))
                self.t(lambda e, h=h, b2=b2, off=off, Z=Z: e.matmul(
                    b2[:64, off:off + 64], Z[:, 64 * h:64 * h + 64],
                    self.s2nT[:, 64 * h:64 * h + 64], start=True, stop=True),
                    inc=(sPE if h == H - 1 else None))
            for bi in range(2):
                self.v(lambda e, bi=bi: e.tensor_sub(
                    self.xs2[:, 512 * bi:512 * (bi + 1)],
                    self.i7[:, 512 * bi:512 * (bi + 1)], pb[4 + bi][:64, :]),
                    waits=[(sPE, sPE.n)] + ([(sX, ipack_ready)]
                                            if (it == 0 and bi == 0) else []))
                self.v(lambda e, bi=bi: e.tensor_copy(
                    self.xs1[:, 512 * bi:512 * (bi + 1)], pb[6 + bi][:64, :]),
                    inc=(sVE if bi == 1 else None))
            if it == 0:
                self.dump_now('xt0D', self.xs1[:, :], [64, H * M], F32)
                self.dump_now('b0D', self.xs2[:, :], [64, H * M], F32)
            for h in range(H):
                b1 = pb[4 + h // 8]; off = 64 * (h % 8)
                self.t(lambda e, h=h, b1=b1, off=off: e.matmul(
                    b1[:64, off:off + 64], self.xs1[:, 64 * h:64 * h + 64],
                    self.xs2[:, 64 * h:64 * h + 64], start=True, stop=True),
                    waits=([(sVE, sVE.n)] if h == 0 else []),
                    inc=(sPE if h == H - 1 else None))
            if it == 0:
                self.dump_now('a2pre', self.xs2[:, :], [64, H * M], F32)
            for bi in range(2):
                self.v(lambda e, bi=bi: e.tensor_sub(
                    self.xs2[:, 512 * bi:512 * (bi + 1)],
                    self.i15[:, 512 * bi:512 * (bi + 1)], pb[4 + bi][:64, :]),
                    waits=[(sPE, sPE.n)], inc=(sVE if bi == 1 else None))
            if it == 0:
                self.dump_now('a2D', self.xs2[:, :], [64, H * M], F32)
            for h in range(H):
                b1 = pb[4 + h // 8]; off = 64 * (h % 8)
                self.t(lambda e, h=h, b1=b1, off=off: e.matmul(
                    b1[:64, off:off + 64], self.xs1[:, 64 * h:64 * h + 64],
                    self.xs2[:, 64 * h:64 * h + 64], start=True, stop=True),
                    waits=([(sVE, sVE.n)] if h == 0 else []),
                    inc=(sPE if h == H - 1 else None))
            for bi in range(2):
                self.v(lambda e, bi=bi: e.tensor_sub(
                    self.xs2[:, 512 * bi:512 * (bi + 1)],
                    self.i13[:, 512 * bi:512 * (bi + 1)], pb[4 + bi][:64, :]),
                    waits=[(sPE, sPE.n)], inc=(sVE if bi == 1 else None))
            if it == 0:
                self.dump_now('mD', self.xs2[:, :], [64, H * M], F32)
            for h in range(H):
                b1 = pb[4 + h // 8]; b2 = pb[6 + h // 8]; off = 64 * (h % 8)
                self.t(lambda e, h=h, b1=b1, off=off, ZT=ZT: e.matmul(
                    b1[:64, off:off + 64], ZT[:, 64 * h:64 * h + 64],
                    self.xs2[:, 64 * h:64 * h + 64], start=True, stop=True),
                    waits=([(sVE, sVE.n)] if h == 0 else []))
                self.t(lambda e, h=h, b2=b2, off=off, ZT=ZT: e.matmul(
                    b2[:64, off:off + 64], self.xs2[:, 64 * h:64 * h + 64],
                    ZT[:, 64 * h:64 * h + 64], start=True, stop=True),
                    inc=(sPE if h == H - 1 else None))
            for bi in range(2):
                self.v(lambda e, bi=bi, Z=Z: e.tensor_scalar_mul(
                    Z[:, 512 * bi:512 * (bi + 1)], pb[4 + bi][:64, :], 0.25),
                    waits=[(sPE, sPE.n)])
                self.v(lambda e, bi=bi, ZT=ZT: e.tensor_scalar_mul(
                    ZT[:, 512 * bi:512 * (bi + 1)], pb[6 + bi][:64, :],
                    0.25), inc=(sVE if bi == 1 else None))
            self.dump_now(f'z{it + 1}D', Z[:, :], [64, H * M], F32)
        self.Zfin, self.ZTfin = Z, ZT
        self.pinv_done = sVE.n
        self.dump('ql', self.ql[:, :], [64, H * M], BF16)
        self.dump('kl', self.kl[:, :], [64, H * M], BF16)
        self.dump('s2n', self.s2n[:, :], [64, H * M], F32)
        self.dump('zfin', Z[:, :], [64, H * M], F32)
        self.dump('ztfin', ZT[:, :], [64, H * M], F32)

        self.WkQ = at('WkQ', [128, NKC * 1024], BF16, self.G0)
        wkt_free = [0, 0]
        for h in range(H):
            self.g(lambda e, h=h, b=wkt[h % 2]: e.dma_start(
                b[:, :], self.wkT_d[h, :, :]),
                waits=[(sVE, wkt_free[h % 2])], inc=sW)
            wl = sW.n
            bank = pb[h % 2]
            for kc in range(NKC):
                w = [(sW, wl)]
                if kc == 0:
                    w.append((sVE, sVE.n))
                    if h == 0:
                        w.append((sSC, qlkl))
                self.t(lambda e, h=h, kc=kc, bank=bank, b=wkt[h % 2]:
                       e.matmul(bank[:, 64 * kc:64 * kc + 64],
                                b[:, 128 * kc:128 * (kc + 1)],
                                self.ql[:, 64 * h:64 * h + 64],
                                start=True, stop=True),
                       waits=w, inc=(sPE if kc == NKC - 1 else None))
            self.v(lambda e, h=h, bank=bank: e.tensor_copy(
                self.WkQ[:, :].rearrange('p (kc hm) -> p kc hm', kc=NKC)
                [:, :, 64 * h:64 * h + 64],
                bank[:, :].rearrange('p (kc m) -> p kc m', kc=NKC)),
                waits=[(sPE, sPE.n)], inc=sVE)
            wkt_free[h % 2] = sVE.n
        self.wkq_done = sVE.n
        self.dump('WkQ', self.WkQ[:, :], [128, NKC * 1024], BF16)

    def ph3(self, sb):
        sX, sW, sPE, sVE, sSC = self.sX, self.sW, self.sPE, self.sVE, self.sSC
        pb = self.pb
        E0 = self.E0; at = self.at
        wv = at('wv_sb', [128, NKC * D], BF16, E0 + 4096)
        vaug = [at(f'vaug{i}', [128, H * 65], BF16, E0 + 20480 + 2080 * i)
                for i in range(2)]
        ptm = [at(f'ptm{i}', [128, H * M], BF16, E0 + 24640 + 2048 * i)
               for i in range(2)]
        Ef = at('E_f32', [64, H * HS], F32, self.G0)
        for kc in range(NKC):
            self.g(lambda e, kc=kc: e.dma_start(
                wv[:, D * kc:D * (kc + 1)],
                self.wv_d[128 * kc:128 * (kc + 1), :]),
                waits=([(sPE, sPE.n), (sVE, sVE.n)] if kc == 0 else []),
                inc=sW)
        wv_ready = self.sW.n
        self.g(lambda e: e.memset(
            vaug[0][:, :].rearrange('p (h c) -> p h c', h=H)[:, :, 64:65],
            1.0), k=1, inc=None)
        self.g(lambda e: e.memset(
            vaug[1][:, :].rearrange('p (h c) -> p h c', h=H)[:, :, 64:65],
            1.0), k=1, inc=self.sGP)
        vaug_ones = self.sGP.n

        def emit_E(lc, pbuf, vb, p_done, v_done):
            for h in range(H):
                grp = 0 if h < 7 else (1 if h < 14 else 2)
                ebank = pb[4 + grp]
                eo = ecol(h) % 512
                w = [(sSC, p_done)]
                if h == 0:
                    w.append((sVE, v_done))
                self.t(lambda e, h=h, ebank=ebank, eo=eo, pbuf=pbuf, vb=vb,
                       lc=lc: e.matmul(
                           ebank[:64, eo:eo + 65],
                           pbuf[:, 64 * h:64 * h + 64],
                           vb[:, 65 * h:65 * h + 65],
                           start=(lc == 0 and h in (0, 7, 14)),
                           stop=(lc == 31 and h in (6, 13, 15))),
                       waits=w, inc=(sPE if h == H - 1 else None))

        pendE = None
        vban_free = 0
        sban_free = 0
        for lc in range(32):
            part = lc >= 16
            xn = self.xn1_part if part else self.xn1_own
            t0 = (lc % 16) * 128
            vb = vaug[lc % 2]; pbuf = ptm[lc % 2]
            for jb in range(2):
                bank = pb[jb]
                for kc in range(NKC):
                    w = []
                    if kc == 0:
                        w = [(sVE, vban_free)]
                        if lc == 0 and jb == 0:
                            w += [(sW, wv_ready), (sVE, self.wkq_done),
                                  (self.sGP, vaug_ones)]
                    self.t(lambda e, xn=xn, t0=t0, kc=kc, jb=jb, bank=bank:
                           e.matmul(bank[:, :],
                                    xn[:, kc * T_OWN + t0:
                                       kc * T_OWN + t0 + 128],
                                    wv[:, D * kc + 512 * jb:
                                       D * kc + 512 * (jb + 1)],
                                    start=(kc == 0), stop=(kc == NKC - 1)),
                           waits=w, inc=(sPE if kc == NKC - 1 else None))
                self.v(lambda e, vb=vb, jb=jb, bank=bank: e.tensor_copy(
                    vb[:, :].rearrange('p (h c) -> p h c', h=H)
                    [:, 8 * jb:8 * (jb + 1), 0:64],
                    bank[:, :].rearrange('p (h m) -> p h m', h=8)),
                    waits=[(sPE, sPE.n)], inc=sVE)
            v_done = sVE.n
            vban_free = v_done
            for jb in range(2):
                bank = pb[2 + jb]
                for kc in range(NKC):
                    w = []
                    if kc == 0:
                        w = [(sSC, sban_free)]
                    self.t(lambda e, xn=xn, t0=t0, kc=kc, jb=jb, bank=bank:
                           e.matmul(bank[:, :],
                                    xn[:, kc * T_OWN + t0:
                                       kc * T_OWN + t0 + 128],
                                    self.WkQ[:, 1024 * kc + 512 * jb:
                                             1024 * kc + 512 * (jb + 1)],
                                    start=(kc == 0), stop=(kc == NKC - 1)),
                           waits=w, inc=(sPE if kc == NKC - 1 else None))
                self.s(lambda e, pbuf=pbuf, jb=jb, bank=bank: e.activation(
                    pbuf[:, 512 * jb:512 * (jb + 1)], bank[:, :], AF.Exp),
                    waits=[(sPE, sPE.n)], inc=sSC)
            p_done = sSC.n
            sban_free = p_done
            if pendE is not None:
                emit_E(*pendE)
            pendE = (lc, pbuf, vb, p_done, v_done)
        emit_E(*pendE)
        e_mm_done = self.sPE.n
        recv = self.recv
        for grp, nh in ((0, 7), (1, 7), (2, 2)):
            bank = pb[4 + grp]
            self.v(lambda e, grp=grp, bank=bank, nh=nh: e.reciprocal(
                recv[:, 7 * grp:7 * grp + nh],
                bank[:64, 0:65 * nh].rearrange('p (h c) -> p h c', c=65)
                [:, :, 64:65]),
                waits=([(sPE, e_mm_done)] if grp == 0 else []))
        self.vsync()
        for h in range(H):
            grp = 0 if h < 7 else (1 if h < 14 else 2)
            bank = pb[4 + grp]
            eo = ecol(h) % 512
            ri = 7 * grp + (h - 7 * grp)
            self.v(lambda e, h=h, bank=bank, eo=eo, ri=ri:
                   e.tensor_scalar_mul(
                       Ef[:, 64 * h:64 * h + 64],
                       bank[:64, eo:eo + 64], recv[:, ri:ri + 1]),
                   inc=(sVE if h == H - 1 else None))
        self.vsync()
        E_done = sVE.n
        for h in range(H):
            bank = pb[h % 2]
            self.t(lambda e, h=h, bank=bank: e.matmul(
                bank[:64, 64 * (h // 2):64 * (h // 2) + 64],
                self.ZTfin[:, 64 * h:64 * h + 64],
                Ef[:, 64 * h:64 * h + 64], start=True, stop=True),
                waits=([(sVE, E_done)] if h == 0 else []),
                inc=(sPE if h == H - 1 else None))
        for bi in range(2):
            self.v(lambda e, bi=bi: e.tensor_copy(
                self.Wh[:, :].rearrange('p (g two c) -> p g two c', two=2,
                                        c=64)[:, :, bi, :],
                pb[bi][:64, 0:512].rearrange('p (g c) -> p g c', c=64)),
                waits=[(sPE, sPE.n)], inc=sVE)
        self.vsync()
        self.wh_done = sVE.n
        self.dump('E', Ef[:, :], [64, H * HS], F32)
        self.dump('Wh', self.Wh[:, :], [64, H * HS], BF16)

    def ph4(self, sb):
        sX, sW, sPE, sVE, sSC = self.sX, self.sW, self.sPE, self.sVE, self.sSC
        pb = self.pb
        E0 = self.E0; D0 = self.D0; at = self.at
        wo = at('wo_sb', [128, NKC * D], BF16, self.G0)
        qt = at('qt', [128, 8 * 512], BF16, D0)
        qodd = at('qodd', [64, 8 * 512], BF16, D0 + 8192)
        es1 = at('es1', [64, H * 512], BF16, D0 + 16384)
        mrg = at('mrg', [128, 8 * 512], BF16, E0 + 0)
        sumsb = at('sumsb', [16, 1024], BF16, E0 + 8192)
        r1b = at('r1b', [128, 512], F32, E0 + 9216)
        r1b2 = at('r1b2', [128, 512], F32, E0 + 13824)
        self.e16 = at('e16', [64, 256], BF16, E0 + 11264)
        self.b16 = at('b16', [16, 1024], BF16, E0 + 11776)
        self.y(lambda e: e.dma_start(self.e16[:, :], self.d_e16.ap()[:, :]),
               waits=[(sPE, sPE.n), (sVE, sVE.n)], inc=sX)
        self.y(lambda e: e.dma_start(self.b16[:, :], self.d_b16.ap()[:, :]),
               inc=sX)
        p4c_ready = sX.n
        for kc in range(NKC):
            self.g(lambda e, kc=kc: e.dma_start(
                wo[:, D * kc:D * (kc + 1)],
                self.wo_d[128 * kc:128 * (kc + 1), :]),
                waits=([(sPE, sPE.n)] if kc == 0 else []), inc=sW)
        wo_ready = self.sW.n

        r1b3 = at('r1b3', [128, 512], F32, E0 + 15872)
        r1b4 = at('r1b4', [128, 512], F32, E0 + 17920)
        rbt4 = [r1b, r1b2, r1b3, r1b4]
        qev = {0: None, 1: None}
        qev_l = []
        exp_ev = [None] * H
        resid_ev = [None] * NKC
        prev_oh = [None, None]
        for tt in range(4):
            t0 = 512 * tt
            for pr in range(8):
                bank = pb[pr % 2]
                for kc in range(NKC):
                    w = []
                    if kc == 0:
                        if tt == 0 and pr < 2:
                            w = [(sSC, self.sSC.n), (sVE, self.sVE.n)]
                            if pr == 0:
                                w.append((sVE, self.wh_done))
                                w.append((sW, self.wq_ready))
                        else:
                            w = [qev[pr % 2]]
                    self.t(lambda e, pr=pr, kc=kc, bank=bank, t0=t0:
                           e.matmul(bank[:, :],
                                    self.wq[:, D * kc + 128 * pr:
                                            D * kc + 128 * (pr + 1)],
                                    self.xn1_own[:, kc * T_OWN + t0:
                                                 kc * T_OWN + t0 + 512],
                                    start=(kc == 0), stop=(kc == NKC - 1)),
                           waits=w, inc=(sPE if kc == NKC - 1 else None))
                wv = [(sPE, sPE.n)]
                if pr == 0 and tt > 0:
                    wv.append((sX, qsplit))
                self.v(lambda e, pr=pr, bank=bank: e.tensor_scalar_mul(
                    qt[:, 512 * pr:512 * (pr + 1)], bank[:, :], 0.125),
                    waits=wv, inc=sVE)
                qev[pr % 2] = (sVE, sVE.n)
                qev_l.append((sVE, sVE.n))
            self.y(lambda e: e.dma_start(
                qodd[:, :].rearrange('p (pr c) -> p pr c', pr=8),
                qt[64:128, :].rearrange('p (pr c) -> p pr c', pr=8)),
                waits=[(sVE, sVE.n)], inc=sX)
            qsplit = sX.n
            for h in range(H):
                bank = pb[2 + h % 2]
                if h % 2 == 0:
                    qsrc = qt[0:64, 512 * (h // 2):512 * (h // 2) + 512]
                    w = [qev_l[-8 + h // 2]]
                else:
                    qsrc = qodd[:, 512 * (h // 2):512 * (h // 2) + 512]
                    w = [(sX, qsplit)]
                if h >= 2:
                    w.append(exp_ev[h - 2])
                elif tt > 0:
                    w.append(resid_ev[6 + h])
                self.t(lambda e, h=h, bank=bank, qsrc=qsrc: e.matmul(
                    bank[:64, :], self.kl[:, 64 * h:64 * h + 64], qsrc,
                    start=True, stop=True), waits=w,
                    inc=sPE)
                self.s(lambda e, h=h, bank=bank: e.activation(
                    es1[:, 512 * h:512 * (h + 1)], bank[:64, :], AF.Exp),
                    waits=[(sPE, sPE.n)], inc=sSC)
                exp_ev[h] = (sSC, sSC.n)
            for h in range(H):
                w = [exp_ev[h]]
                if h == 0:
                    if tt == 0:
                        w += [(sVE, self.sVE.n), (sX, p4c_ready)]
                    else:
                        w += [prev_oh[0]]
                self.t(lambda e, h=h: e.matmul(
                    pb[4][:16, :], self.e16[:, 16 * h:16 * (h + 1)],
                    es1[:, 512 * h:512 * (h + 1)],
                    start=(h == 0), stop=(h == H - 1)),
                    waits=w, inc=(sPE if h == H - 1 else None))
            self.v(lambda e: e.tensor_copy(sumsb[:16, 0:512], pb[4][:16, :]),
                   waits=[(sPE, sPE.n)], inc=sVE)
            sums_done = (sVE, sVE.n)
            rec_ev = [None, None]
            mult_ev = [None, None]

            def bcast_rec(pr):
                bb = pb[5 + 2 * (pr % 2)]
                w = [sums_done] if pr < 2 else [rec_ev[pr % 2]]
                self.t(lambda e, pr=pr, bb=bb: e.matmul(
                    bb[:, :], self.b16[:, 128 * pr:128 * (pr + 1)],
                    sumsb[:16, 0:512], start=True, stop=True),
                    waits=w, inc=sPE)
                self.v(lambda e, pr=pr, bb=bb: e.reciprocal(
                    rbt4[pr % 4][:, :], bb[:, :]),
                    waits=[(sPE, sPE.n)], inc=sVE)
                rec_ev[pr % 2] = (sVE, sVE.n)

            def outh_mult(pr):
                ob = pb[4 + 2 * (pr % 2)]
                for par in range(2):
                    h = 2 * pr + par
                    w = []
                    if par == 0:
                        if pr < 2:
                            if tt > 0:
                                w = [prev_oh[pr % 2]]
                        else:
                            w = [mult_ev[pr % 2]]
                        if pr == 0:
                            w = w + [sums_done]
                    self.t(lambda e, h=h, par=par, ob=ob: e.matmul(
                        ob[64 * par:64 * par + 64, :],
                        self.Wh[:, 64 * h:64 * h + 64],
                        es1[:, 512 * h:512 * (h + 1)],
                        start=True, stop=True),
                        waits=w, inc=(sPE if par == 1 else None))
                self.v(lambda e, pr=pr, ob=ob: e.tensor_mul(
                    mrg[:, 512 * pr:512 * (pr + 1)], ob[:, :],
                    rbt4[pr % 4][:, :]),
                    waits=[(sPE, sPE.n)], inc=sVE)
                mult_ev[pr % 2] = (sVE, sVE.n)

            bcast_rec(0)
            bcast_rec(1)
            bcast_rec(2)
            bcast_rec(3)
            outh_mult(0)
            outh_mult(1)
            bcast_rec(4)
            bcast_rec(5)
            outh_mult(2)
            outh_mult(3)
            bcast_rec(6)
            bcast_rec(7)
            outh_mult(4)
            outh_mult(5)
            outh_mult(6)
            outh_mult(7)
            prev_oh = [mult_ev[0], mult_ev[1]]
            mrg_done = self.sVE.n
            for oc in range(NKC):
                bank = pb[2 + oc % 2]
                for kc in range(NKC):
                    w = []
                    if kc == 0:
                        w = [(sVE, mrg_done)]
                        if oc >= 2:
                            w = [resid_ev[oc - 2]]
                        if tt == 0 and oc == 0:
                            w.append((sW, wo_ready))
                    self.t(lambda e, oc=oc, kc=kc, bank=bank: e.matmul(
                        bank[:, :], wo[:, D * kc + 128 * oc:
                                       D * kc + 128 * (oc + 1)],
                        mrg[:, 512 * kc:512 * (kc + 1)],
                        start=(kc == 0), stop=(kc == NKC - 1)),
                        waits=w, inc=(sPE if kc == NKC - 1 else None))
                self.v(lambda e, oc=oc, t0=t0, bank=bank:
                       e.scalar_tensor_tensor(
                           self.resid[:, oc * T_OWN + t0:
                                      oc * T_OWN + t0 + 512],
                           bank[:, :], self.bo_sb[:, oc:oc + 1],
                           self.resid[:, oc * T_OWN + t0:
                                      oc * T_OWN + t0 + 512],
                           ALU.add, ALU.add),
                       waits=[(sPE, sPE.n)], inc=sVE)
                resid_ev[oc] = (sVE, sVE.n)
        self.r1_done = self.sVE.n
        self.dump('r1', self.resid[:, :], [128, NKC * T_OWN], F32)

    def ph5(self, sb):
        sPE, sVE, sSC = self.sPE, self.sVE, self.sSC
        pb = self.pb
        at = self.at; C0 = self.C0; E0 = self.E0
        self.xn3 = at('xn3', [128, NKC * T_OWN], BF16, E0)
        hib = [at(f'hib_p5{i}', [128, 512], BF16, C0 + 1024 * i)
               for i in range(2)]
        sqb = at('sqb_p5', [128, 512], BF16, C0 + 2048)
        mb = [at(f'mb_p5{i}', [128, 512], F32, C0 + 3072 + 2048 * i)
              for i in range(2)]
        ra = [at(f'ra_p5{i}', [128, 512], F32, C0 + 7168 + 2048 * i)
              for i in range(2)]
        t1 = at('t1_p5', [128, 512], F32, C0 + 11264)
        rb = [at(f'rb_p5{i}', [128, 512], F32, C0 + 18432 + 2048 * i)
              for i in range(2)]
        tmp = [at(f'tmp_p5{i}', [128, 512], F32, C0 + 22528 + 2048 * i)
               for i in range(2)]
        ug = [at(f'ug_p5{i}', [128, 512], F32, C0 + 26624 + 2048 * i)
              for i in range(2)]
        ua = at('ua_p5', [128, 512], F32, C0 + 30720)
        stats_free = [0, 0]
        stat_ready = [None] * 4
        ug_free = []

        def stats(tt):
            t0 = 512 * tt
            par = tt % 2
            pba, pbb_ = pb[0 + 2 * par], pb[1 + 2 * par]
            hb = hib[par]
            for kc in range(NKC):
                w = [(sVE, self.r1_done)] if (tt == 0 and kc == 0) else \
                    [(sPE, sPE.n)]
                self.s(lambda e, kc=kc, t0=t0, hb=hb: e.copy(
                    hb[:, :], self.resid[:, kc * T_OWN + t0:
                                         kc * T_OWN + t0 + 512]),
                    waits=w, inc=sSC)
                self.s(lambda e, hb=hb: e.activation(
                    sqb[:, :], hb[:, :], AF.Square), inc=sSC)
                sq_sc = sSC.n
                self.t(lambda e, kc=kc, hb=hb, pba=pba: e.matmul(
                    pba[:, :], self.onesb[:, :], hb[:, :],
                    start=(kc == 0), stop=(kc == NKC - 1)),
                    waits=[(sSC, sq_sc)] + ([(sVE, stats_free[par])]
                                            if kc == 0 else []))
                self.t(lambda e, kc=kc, pbb_=pbb_: e.matmul(
                    pbb_[:, :], self.onesb[:, :], sqb[:, :],
                    start=(kc == 0), stop=(kc == NKC - 1)),
                    waits=[(sSC, sq_sc)], inc=sPE)
            mm_done = sPE.n
            mbp, rbp, rap, tmpp = mb[par], rb[par], ra[par], tmp[par]
            self.v(lambda e, pba=pba, mbp=mbp: e.tensor_scalar_mul(
                mbp[:, :], pba[:, :], 1.0 / D), waits=[(sPE, mm_done)])
            self.v(lambda e, mbp=mbp, tmpp=tmpp: e.tensor_mul(
                tmpp[:, :], mbp[:, :], mbp[:, :]), inc=sVE)
            self.s(lambda e, tmpp=tmpp: e.add(
                tmpp[:, :], tmpp[:, :], self.epsc[:, 1:2]),
                waits=[(sVE, sVE.n)], inc=sSC)
            self.v(lambda e, pbb_=pbb_, rbp=rbp, tmpp=tmpp:
                   e.scalar_tensor_tensor(
                       rbp[:, :], pbb_[:, :], 1.0 / D, tmpp[:, :],
                       ALU.mult, ALU.subtract), waits=[(sSC, sSC.n)],
                   inc=sVE)
            stats_free[par] = sVE.n
            self.s(lambda e, rbp=rbp, tmpp=tmpp: e.activation(
                tmpp[:, :], rbp[:, :], AF.Ln), waits=[(sVE, sVE.n)])
            self.s(lambda e, rap=rap, tmpp=tmpp: e.activation(
                rap[:, :], tmpp[:, :], AF.Exp, scale=-0.5))
            self.s(lambda e, rbp=rbp, tmpp=tmpp: e.add(
                tmpp[:, :], rbp[:, :], self.epsc[:, 1:2]), inc=sSC)
            self.v(lambda e, rap=rap, tmpp=tmpp: e.tensor_mul(
                tmpp[:, :], tmpp[:, :], rap[:, :]),
                waits=[(sSC, sSC.n)])
            self.v(lambda e, rap=rap, tmpp=tmpp: e.tensor_mul(
                tmpp[:, :], tmpp[:, :], rap[:, :]), inc=sVE)
            self.s(lambda e, tmpp=tmpp: e.add(
                tmpp[:, :], tmpp[:, :], self.epsc[:, 0:1]),
                waits=[(sVE, sVE.n)])
            self.s(lambda e, tmpp=tmpp: e.activation(
                tmpp[:, :], tmpp[:, :], AF.Ln))
            self.s(lambda e, tmpp=tmpp: e.activation(
                tmpp[:, :], tmpp[:, :], AF.Exp, scale=-0.5), inc=sSC)
            self.v(lambda e, rbp=rbp, rap=rap, tmpp=tmpp: e.tensor_mul(
                rbp[:, :], rap[:, :], tmpp[:, :]),
                waits=[(sSC, sSC.n)], inc=sVE)
            stat_ready[tt] = sVE.n

        def apply(tt):
            t0 = 512 * tt
            par = tt % 2
            mbp, rbp, rap = mb[par], rb[par], ra[par]
            for kc in range(NKC):
                col = kc * T_OWN + t0
                kp = kc % 2
                ugp = ug[kp]
                w = [(sVE, stat_ready[tt])] if kc == 0 else []
                if len(ug_free) >= 2:
                    w = w + [(sSC, ug_free[-2])]
                self.v(lambda e, col=col, mbp=mbp: e.tensor_sub(
                    t1[:, :], self.resid[:, col:col + 512], mbp[:, :]),
                    waits=w)
                self.v(lambda e, rbp=rbp, ugp=ugp: e.tensor_mul(
                    ugp[:, :], t1[:, :], rbp[:, :]), inc=sVE)
                svv = sVE.n
                self.v(lambda e, rap=rap: e.tensor_mul(
                    ua[:, :], t1[:, :], rap[:, :]))
                self.v(lambda e, col=col: e.tensor_add(
                    self.resid[:, col:col + 512],
                    self.resid[:, col:col + 512], ua[:, :]), inc=sVE)
                self.s(lambda e, kc=kc, ugp=ugp: e.mul(
                    tmp[par][:, :], ugp[:, :], self.gam[:, kc:kc + 1]),
                    waits=[(sVE, svv)])
                self.s(lambda e, kc=kc, col=col: e.add(
                    self.xn3[:, col:col + 512], tmp[par][:, :],
                    self.bet[:, kc:kc + 1]), inc=sSC)
                ug_free.append(sSC.n)

        stats(0)
        stats(1)
        apply(0)
        stats(2)
        apply(1)
        stats(3)
        apply(2)
        apply(3)
        self.xn3_done = self.sVE.n
        self.dump('xn3', self.xn3[:, :], [128, NKC * T_OWN], BF16)
        self.dump('acc5', self.resid[:, :], [128, NKC * T_OWN], F32)

    def ph6(self, sb):
        sX, sW, sPE, sVE, sSC = self.sX, self.sW, self.sPE, self.sVE, self.sSC
        sGU, sDN = self.sGU, self.sDN
        pb = self.pb
        at = self.at; G0 = self.G0; C0 = self.C0
        gub = [at(f'gub{i}', [128, 2048], BF16, G0 + 4096 * i)
               for i in range(3)]
        grow = [at('grow0', [1, 2048], BF16, 215648),
                at('grow1', [1, 2048], BF16, 219744)]
        RB = C0 + 14336
        m2scr = at('m2scr', [128, 16 * 28], F32, RB)
        logit = at('logit', [128, 128], F32, RB + 1824)
        gT = at('gT', [128, 128], F32, 228288)
        rt = at('rt_sb', [128, NKC * NE], BF16, RB + 2848)
        stat = at('stat_p6', [128, 16], F32, RB + 2976)
        stat2 = at('stat2_p6', [128, 16], F32, RB + 3040)
        m2t = at('m2t', [128, 16], F32, RB + 3104)
        gtile = at('gtile', [128, 128], F32, RB + 3168)
        hid = at('hid', [128, 16 * 1024], BF16, C0)
        xe = at('xe', [128, NKC * 1024], BF16, self.F0)
        dn = at('dn', [128, 16 * 1024], BF16, self.D0)

        self.g(lambda e: e.dma_start(rt[:, :], self.routerT_d[:, :]),
            waits=[(sPE, sPE.n)], inc=sW)
        rt_ready = sW.n

        for tc in range(16):
            bank = pb[2]
            for kc in range(NKC):
                w = []
                if kc == 0:
                    w = [(sVE, self.xn3_done), (sSC, sSC.n)]
                    if tc == 0:
                        w.append((sW, rt_ready))
                self.t(lambda e, tc=tc, kc=kc, bank=bank: e.matmul(
                    bank[:, 8 * tc:8 * tc + 8],
                    self.xn3[:, kc * T_OWN + 128 * tc:
                             kc * T_OWN + 128 * tc + 128],
                    rt[:, NE * kc:NE * (kc + 1)],
                    start=(kc == 0), stop=(kc == NKC - 1)),
                    waits=w, inc=(sPE if kc == NKC - 1 else None))
            if tc == 7 or tc == 15:
                self.s(lambda e, tc=tc, bank=bank: e.copy(
                    logit[:, 64 * (tc // 8):64 * (tc // 8) + 64],
                    bank[:, 64 * (tc // 8):64 * (tc // 8) + 64]),
                    waits=[(sPE, sPE.n)], inc=sSC)
        lg = sSC.n
        self.v(lambda e: e.tensor_tensor(
            logit[:, :], logit[:, :], self.clipp[:, 0:128], ALU.min),
            waits=[(sSC, lg)])
        self.v(lambda e: e.tensor_tensor(
            logit[:, :], logit[:, :], self.clipp[:, 128:256], ALU.max),
            inc=sVE)
        self.vsync()
        self.v(lambda e: e.tensor_reduce(
            stat[:, :], logit[:, :].rearrange('p (c e) -> p c e', c=16),
            AX.X, ALU.max, negate=True))
        self.vsync()
        for c in range(16):
            self.s(lambda e, c=c: e.activation(
                logit[:, 8 * c:8 * c + 8], logit[:, 8 * c:8 * c + 8],
                AF.Exp, bias=stat[:, c:c + 1]),
                waits=([(sVE, sVE.n)] if c == 0 else []),
                inc=(sSC if c == 15 else None))
        self.v(lambda e: e.tensor_reduce(
            stat2[:, :], logit[:, :].rearrange('p (c e) -> p c e', c=16),
            AX.X, ALU.add), waits=[(sSC, sSC.n)])
        self.vsync()
        self.v(lambda e: e.tensor_scalar_add(stat2[:, :], stat2[:, :], REPS))
        self.vsync()
        self.v(lambda e: e.reciprocal(stat2[:, :], stat2[:, :]), inc=sVE)
        self.vsync()
        sv_r = sVE.n
        for c in range(16):
            self.s(lambda e, c=c: e.mul(
                logit[:, 8 * c:8 * c + 8], logit[:, 8 * c:8 * c + 8],
                stat2[:, c:c + 1]),
                waits=([(sVE, sv_r)] if c == 0 else []),
                inc=(sSC if c == 15 else None))
        self.wait('vector', sSC)
        self.v(lambda e: e.tensor_tensor(
            logit[:, :], logit[:, :], self.clipp[:, 256:384], ALU.min))
        self.v(lambda e: e.tensor_tensor(
            logit[:, :], logit[:, :], self.clipp[:, 384:512], ALU.max),
            inc=sVE)
        self.vsync()
        self.s(lambda e: e.copy(
            gtile[:, :].rearrange('p (e c) -> p c e', e=NE),
            logit[:, :].rearrange('p (c e) -> p c e', c=16)),
            waits=[(sVE, sVE.n)], inc=sSC)
        self.wait('vector', sSC)
        k = 0
        for i in range(NE):
            for j in range(i + 1, NE):
                self.v(lambda e, i=i, j=j, k=k: e.tensor_tensor(
                    m2scr[:, :].rearrange('p (c s) -> p c s', s=28)
                    [:, :, k:k + 1],
                    gtile[:, 16 * i:16 * i + 16],
                    gtile[:, 16 * j:16 * j + 16], ALU.min))
                k += 1
        self.vsync()
        self.v(lambda e: e.tensor_reduce(
            m2t[:, :], m2scr[:, :].rearrange('p (c s) -> p c s', s=28),
            AX.X, ALU.max))
        self.vsync()
        for ei in range(NE):
            self.v(lambda e, ei=ei: e.tensor_tensor(
                logit[:, 16 * ei:16 * (ei + 1)],
                gtile[:, 16 * ei:16 * (ei + 1)], m2t[:, :], ALU.is_ge))
        self.vsync()
        for ei in range(NE):
            self.v(lambda e, ei=ei: e.tensor_tensor(
                gtile[:, 16 * ei:16 * (ei + 1)],
                gtile[:, 16 * ei:16 * (ei + 1)],
                logit[:, 16 * ei:16 * (ei + 1)], ALU.mult),
                inc=(sVE if ei == NE - 1 else None))
        self.vsync()
        self.t(lambda e: e.transpose(pb[3][:, :128], gtile[:, :],
                                     self.idf[:, :]),
               waits=[(sVE, sVE.n)], inc=sPE)
        self.v(lambda e: e.tensor_copy(gT[:, :], pb[3][:, :128]),
               waits=[(sPE, sPE.n)], inc=sVE)
        self.vsync()
        gT_done = sVE.n
        self.dump('gT', gT[:, :], [128, 128], F32)

        sched = [(ei, th, p) for ei in range(NE) for th in range(2)
                 for p in range(16)]
        slice_pe = {}
        slice_dma = {}
        gate_ev = [0, 0]
        up_ev = [0, 0]
        dn_loaded = 0
        xe_done = 0
        grow_ready = 0
        acc_ev = [0, 0, 0, 0]

        def issue_gu(s_idx):
            ei, th, p = sched[s_idx]
            gbuf = gub[s_idx % 3]
            w = []
            if s_idx >= 3:
                w.append((sPE, slice_pe[s_idx - 3]))
            self.g(lambda e, ei=ei, p=p, gbuf=gbuf: e.dma_start(
                gbuf[:, :].rearrange('p (kc j) -> p kc j', kc=NKC),
                self.WG_d[ei, p, :, :].rearrange(
                    'p (kc j) -> p kc j', kc=NKC)),
                waits=w, inc=sGU)
            slice_dma[s_idx] = sGU.n

        for s in range(3):
            issue_gu(s)

        units = [(ei, th) for ei in range(NE) for th in range(2)]
        grow_rdy = {}

        def issue_grow(ei):
            self.g(lambda e, ei=ei: e.dma_start(
                grow[ei % 2][0:1, :], gT[16 * ei:16 * (ei + 1), :]),
                waits=[(sVE, gT_done), (sPE, sPE.n)], inc=sW)
            grow_rdy[ei] = sW.n

        def emit_xe(u):
            ei, th = units[u]
            th0 = 1024 * th
            for ts in range(2):
                tso = th0 + 512 * ts
                self.t(lambda e, tso=tso, ts=ts, ei=ei: e.matmul(
                    pb[6 + ts][:, :], self.onesrow[:1, :],
                    grow[ei % 2][:1, tso:tso + 512], start=True, stop=True),
                    waits=[(sW, grow_rdy[ei]), (sVE, sVE.n)], inc=sPE)
            bc_pe = sPE.n
            for ts in range(2):
                for kc in range(NKC):
                    self.v(lambda e, kc=kc, ts=ts, th0=th0:
                           e.tensor_mul(
                               xe[:, 1024 * kc + 512 * ts:
                                  1024 * kc + 512 * ts + 512],
                               self.xn3[:, kc * T_OWN + th0 + 512 * ts:
                                        kc * T_OWN + th0 + 512 * ts + 512],
                               pb[6 + ts][:, :]),
                           waits=([(sPE, bc_pe)]
                                  if (ts == 0 and kc == 0) else []),
                           inc=(sVE if (ts == 1 and kc == NKC - 1)
                                else None))
            return sVE.n

        issue_grow(0)
        xe_done = emit_xe(0)
        for u, (ei, th) in enumerate(units):
            th0 = 1024 * th
            if th == 0:
                dn_pending = list(range(8))
                dn_wait = (sPE, self.sPE.n)
            if th == 1 and ei + 1 < NE:
                issue_grow(ei + 1)
            for p in range(16):
                s_idx = u * 16 + p
                gbuf = gub[s_idx % 3]
                for ts in range(2):
                    gslot = (2 * s_idx + ts) % 2
                    gbank = pb[0 + 2 * gslot]
                    ubank = pb[1 + 2 * gslot]
                    for kc in range(NKC):
                        w = []
                        if kc == 0:
                            w = [(sGU, slice_dma[s_idx]),
                                 (sVE, xe_done if (p == 0 and ts == 0)
                                  else up_ev[gslot]),
                                 (sSC, gate_ev[gslot])]
                        self.t(lambda e, kc=kc, ts=ts, gbuf=gbuf,
                               gbank=gbank:
                               e.matmul(gbank[:, :],
                                        gbuf[:, 256 * kc:256 * kc + 128],
                                        xe[:, 1024 * kc + 512 * ts:
                                           1024 * kc + 512 * ts + 512],
                                        start=(kc == 0),
                                        stop=(kc == NKC - 1)), waits=w)
                        self.t(lambda e, kc=kc, ts=ts, gbuf=gbuf,
                               ubank=ubank:
                               e.matmul(ubank[:, :],
                                        gbuf[:, 256 * kc + 128:
                                             256 * kc + 256],
                                        xe[:, 1024 * kc + 512 * ts:
                                           1024 * kc + 512 * ts + 512],
                                        start=(kc == 0),
                                        stop=(kc == NKC - 1)),
                               inc=(sPE if kc == NKC - 1 else None))
                    slice_pe[s_idx] = sPE.n
                    self.s(lambda e, p=p, ts=ts, gbank=gbank:
                           e.activation(
                               hid[:, 1024 * p + 512 * ts:
                                   1024 * p + 512 * ts + 512],
                               gbank[:, :], AF.Silu),
                           waits=[(sPE, sPE.n)], inc=sSC)
                    gate_ev[gslot] = sSC.n
                    self.v(lambda e, p=p, ts=ts, ubank=ubank:
                           e.tensor_mul(
                               hid[:, 1024 * p + 512 * ts:
                                   1024 * p + 512 * ts + 512],
                               hid[:, 1024 * p + 512 * ts:
                                   1024 * p + 512 * ts + 512],
                               ubank[:, :]),
                           waits=[(sSC, sSC.n)], inc=sVE)
                    up_ev[gslot] = sVE.n
                if s_idx + 3 < len(sched):
                    issue_gu(s_idx + 3)
                if th == 0 and dn_pending:
                    hc2 = dn_pending.pop(0)
                    self.g(lambda e, ei=ei, hc2=hc2: e.dma_start(
                        dn[:, 2048 * hc2:2048 * (hc2 + 1)].rearrange(
                            'p (hc dd) -> p hc dd', hc=2),
                        self.DN_d[ei, 2 * hc2:2 * (hc2 + 1), :, :].rearrange(
                            'hc p dd -> p hc dd')),
                        waits=[dn_wait] if hc2 == 0 else [], inc=sDN)
                    if not dn_pending:
                        dn_loaded = sDN.n
            hid_done = sVE.n
            nxt_xe = None
            if u + 1 < len(units):
                nxt_xe = emit_xe(u + 1)
            di = 0
            for dc in range(NKC):
                for ts in range(2):
                    bank = pb[4 + di % 4]
                    for hc in range(16):
                        w = []
                        if hc == 0:
                            w = [(sVE, hid_done), (sDN, dn_loaded),
                                 (sVE, acc_ev[di % 4])]
                            if di % 4 >= 2 and nxt_xe is not None:
                                w.append((sVE, nxt_xe))
                        self.t(lambda e, dc=dc, hc=hc, ts=ts, bank=bank:
                               e.matmul(bank[:, :],
                                        dn[:, 1024 * hc + 128 * dc:
                                           1024 * hc + 128 * (dc + 1)],
                                        hid[:, 1024 * hc + 512 * ts:
                                            1024 * hc + 512 * ts + 512],
                                        start=(hc == 0),
                                        stop=(hc == 15)),
                               waits=w, inc=(sPE if hc == 15 else None))
                    col = dc * T_OWN + th0 + 512 * ts
                    self.v(lambda e, col=col, bank=bank: e.tensor_add(
                        self.resid[:, col:col + 512],
                        self.resid[:, col:col + 512], bank[:, :]),
                        waits=[(sPE, sPE.n)], inc=sVE)
                    acc_ev[di % 4] = sVE.n
                    di += 1
            xe_done = nxt_xe if nxt_xe is not None else xe_done
        self.moe_done = self.sVE.n
        self.dump('acc6', self.resid[:, :], [128, NKC * T_OWN], F32)

    def ph7(self, sb):
        sX, sPE, sVE = self.sX, self.sPE, self.sVE
        pb = self.pb
        ot = [self.at(f'ot{i}', [128, 2 * D], F32, self.D0 + 8192 * i)
              for i in range(2)]
        evac_cnt = {}
        store_cnt = [0, 0]
        for tc in range(16):
            obuf = ot[(tc // 2) % 2]
            ocol = (tc % 2) * D
            for kc in range(NKC):
                bank = pb[kc % 4]
                w = []
                if tc == 0 and kc == 0:
                    w.append((sVE, self.moe_done))
                if kc % 4 in evac_cnt:
                    w.append((sVE, evac_cnt[kc % 4]))
                self.t(lambda e, tc=tc, kc=kc, bank=bank: e.transpose(
                    bank[:, :128],
                    self.resid[:, kc * T_OWN + 128 * tc:
                               kc * T_OWN + 128 * (tc + 1)],
                    self.idf[:, :]),
                    waits=w, inc=sPE)
                self.v(lambda e, obuf=obuf, ocol=ocol, kc=kc, bank=bank:
                       e.tensor_copy(obuf[:, ocol + 128 * kc:
                                          ocol + 128 * (kc + 1)],
                                     bank[:, :128]),
                       waits=[(sPE, sPE.n)] +
                             ([(sX, store_cnt[(tc // 2) % 2])]
                              if kc == 0 else []),
                       inc=sVE)
                evac_cnt[kc % 4] = sVE.n
            if tc % 2 == 1:
                self.vsync()
                self.y(lambda e, tc=tc, obuf=obuf: e.dma_start(
                    self.out_d[128 * (tc - 1):128 * (tc + 1), :].rearrange(
                        '(c p) f -> p c f', p=128),
                    obuf[:, :].rearrange('p (c f) -> p c f', c=2)),
                    waits=[(sVE, sVE.n)], inc=self.sST)
                store_cnt[(tc // 2) % 2] = self.sST.n
        self.wait('sync', self.sST)


def host_prep(inputs):
    gu = inputs['L0_ffn_gate_up_w']
    dn = inputs['L0_ffn_down_w']
    WG = np.empty((NE, 16, 128, 2048), np.float32)
    for e in range(NE):
        T = np.ascontiguousarray(gu[e].T)
        gate = T[:, :2048].reshape(1024, 16, 128)
        up = T[:, 2048:].reshape(1024, 16, 128)
        paired = np.concatenate([gate, up], axis=2)
        arr = paired.reshape(8, 128, 16, 256).transpose(2, 1, 0, 3)
        WG[e] = arr.reshape(16, 128, 2048)
    DN = np.empty((NE, 16, 128, D), np.float32)
    for e in range(NE):
        T = np.ascontiguousarray(dn[e].T)
        DN[e] = T.reshape(16, 128, D)
    wkT16 = np.ascontiguousarray(
        inputs['L0_attn_Wk'].T.reshape(H, HS, D)).astype(np.float32)
    rT = inputs['L0_ffn_router_w'].T.astype(np.float32)
    routerT = np.ascontiguousarray(
        rT.reshape(8, 128, 8).transpose(1, 0, 2).reshape(128, 64))
    shared = {
        'wq': np.ascontiguousarray(inputs['L0_attn_Wq']),
        'wk': np.ascontiguousarray(inputs['L0_attn_Wk']),
        'wv': np.ascontiguousarray(inputs['L0_attn_Wv']),
        'wo': np.ascontiguousarray(inputs['L0_attn_Wo']),
        'bo': np.ascontiguousarray(inputs['L0_attn_bo']),
        'wkT16': wkT16,
        'routerT': routerT,
        'gamma': np.ascontiguousarray(inputs['L0_ffn_ln_gamma']),
        'beta': np.ascontiguousarray(inputs['L0_ffn_ln_beta']),
        'WG': WG, 'DN': DN,
    }
    x = inputs['L0_attn_x']
    in_maps = []
    for c in range(8):
        b, hh = c // 2, c % 2
        im = dict(shared)
        im['x_own'] = np.ascontiguousarray(x[b, hh * 2048:(hh + 1) * 2048])
        im['x_part'] = np.ascontiguousarray(
            x[b, (1 - hh) * 2048:(2 - hh) * 2048])
        in_maps.append(im)
    return in_maps


def kernel(**inputs):
    in_maps = host_prep(inputs)
    kb = KB(upto=7)
    nc = kb.build()
    from concourse.bass_utils import run_bass_kernel_spmd
    res = run_bass_kernel_spmd(nc, in_maps, core_ids=list(range(8)))
    out = np.empty((4, 4096, 1024), np.float32)
    for c in range(8):
        b, hh = c // 2, c % 2
        out[b, hh * 2048:(hh + 1) * 2048] = np.asarray(
            res.results[c]['out'], np.float32)
    return out


# version 10 (final)
# speedup vs baseline: 1.0457x; 1.0457x over previous
import contextlib
import numpy as np
import ml_dtypes
import concourse.bass as bass
import concourse.mybir as mybir

F32 = mybir.dt.float32
BF16 = mybir.dt.bfloat16
ALU = mybir.AluOpType
AF = mybir.ActivationFunctionType
AX = mybir.AxisListType
bfd = ml_dtypes.bfloat16

T_OWN = 2048
L = 4096
D = 1024
H = 16
HS = 64
M = 64
NKC = 8
NE = 8
EPS = 1e-5
CLAMP = 10000.0
REPS = 1e-8


def ecol(h):
    if h < 7:
        return 65 * h
    if h < 14:
        return 512 + 65 * (h - 7)
    return 1024 + 65 * (h - 14)


class Sem:
    def __init__(self, ctx, nc, name):
        self.h = ctx.enter_context(nc.semaphore(name))
        self.n = 0


class KB:
    def __init__(self, upto=99, dbg=()):
        self.upto = upto
        self.dbg = set(dbg)
        self.nc = bass.Bass('TRN2', target_bir_lowering=False,
                            detect_race_conditions=False)
        self.ctx = contextlib.ExitStack()
        self.ops = {k: [] for k in ("sync", "gpsimd", "tensor", "vector",
                                    "scalar")}
        self.dumps = []

    def _op(self, eng, build, waits, inc, k):
        w = tuple((s.h, int(v)) for (s, v) in waits)
        ih = inc.h if inc is not None else None
        def f(e):
            for hh, vv in w:
                e.wait_ge(hh, vv)
            ins = build(e)
            if ih is not None:
                ins.then_inc(ih, k)
        self.ops[eng].append(f)
        if inc is not None:
            inc.n += k

    def t(self, build, waits=(), inc=None, k=1):
        self._op('tensor', build, waits, inc, k)

    def v(self, build, waits=(), inc=None, k=1):
        self._op('vector', build, waits, inc, k)

    def s(self, build, waits=(), inc=None, k=1):
        self._op('scalar', build, waits, inc, k)

    def g(self, build, waits=(), inc=None, k=16):
        self._op('gpsimd', build, waits, inc, k)

    def y(self, build, waits=(), inc=None, k=16):
        self._op('sync', build, waits, inc, k)

    def vsync(self):
        self.v(lambda e: e.tensor_copy(self.vsync_t[:, 0:4],
                                       self.vsync_t[:, 4:8]))
        self.v(lambda e: e.tensor_copy(self.vsync_t[:, 0:4],
                                       self.vsync_t[:, 4:8]), inc=self.sVE)
        self.wait('vector', self.sVE)

    def wait(self, eng, sem, val=None):
        v = sem.n if val is None else val
        hh = sem.h
        self.ops[eng].append(lambda e: e.wait_ge(hh, int(v)))

    def build(self):
        nc = self.nc
        ctx = self.ctx
        sb = lambda name, shape, dtp, **kw: ctx.enter_context(
            nc.sbuf_tensor(name, shape, dtp, **kw))
        LEFT = dict(side="left"); RIGHT = dict(side="right")
        self.sbf = sb

        dt = lambda n, s: nc.dram_tensor(n, s, F32, kind="ExternalInput").ap()
        self.x_own = dt('x_own', [T_OWN, D])
        self.x_part = dt('x_part', [T_OWN, D])
        self.wq_d = dt('wq', [D, D]); self.wk_d = dt('wk', [D, D])
        self.wv_d = dt('wv', [D, D]); self.wo_d = dt('wo', [D, D])
        self.bo_d = dt('bo', [D])
        self.wkT_d = dt('wkT16', [H, HS, D])
        self.routerT_d = dt('routerT', [128, NKC * NE])
        self.gamma_d = dt('gamma', [D]); self.beta_d = dt('beta', [D])
        self.WG_d = dt('WG', [NE, 16, 128, 2048])
        self.DN_d = dt('DN', [NE, 16, 128, D])
        self.out_d = nc.dram_tensor('out', [T_OWN, D], F32,
                                    kind="ExternalOutput").ap()

        id128f_np = np.eye(128, dtype=np.float32)
        id128b_np = np.eye(128, dtype=bfd)
        ones128_np = np.ones((128, 128), dtype=bfd)
        e16_np = np.zeros((64, 256), dtype=bfd)
        for h in range(16):
            e16_np[:, 16 * h + h] = 1.0
        b16_np = np.zeros((16, 1024), dtype=bfd)
        for p in range(8):
            b16_np[2 * p, 128 * p: 128 * p + 64] = 1.0
            b16_np[2 * p + 1, 128 * p + 64: 128 * p + 128] = 1.0
        def ipack(c):
            t = np.zeros((64, H * 64), dtype=bfd)
            for h in range(H):
                t[:, 64 * h: 64 * h + 64] = c * np.eye(64, dtype=bfd)
            return t
        inl = lambda a, n: nc.inline_tensor(a, n)
        self.d_id128f = inl(id128f_np, 'id128f')
        self.d_id128b = inl(id128b_np, 'id128b')
        self.d_ones = inl(ones128_np, 'ones128')
        self.d_e16 = inl(e16_np, 'e16c')
        self.d_b16 = inl(b16_np, 'b16c')
        self.d_i7 = inl(ipack(7.0), 'i7p')
        self.d_i15 = inl(ipack(15.0), 'i15p')
        self.d_i13 = inl(ipack(13.0), 'i13p')
        self.d_onesrow = inl(np.ones((1, 128), dtype=bfd), 'onesrow')
        self.d_onesf = inl(np.ones((128, 128), np.float32), 'onesf')
        d_epsc = inl((np.full((128, 2), EPS, np.float32) *
                      np.array([1.0, -1.0], np.float32)), 'epsc')
        cp = np.zeros((128, 512), np.float32)
        cp[:, 0:128] = CLAMP; cp[:, 128:256] = -CLAMP
        cp[:, 256:384] = 1.0; cp[:, 384:512] = REPS
        d_clipp = inl(cp, 'clipp')

        at = lambda name, shape, dtp, off: nc.alloc_sbuf_tensor_at(
            name, shape, dtp, offset=off)
        self.at = at
        B0 = 17504
        assert B0 >= nc.sbuf_base + 1000
        self.resid = at('resid', [128, NKC * T_OWN], F32, B0)
        self.idf = at('idf', [128, 128], F32, B0 + 65536)
        self.idb = at('idb', [128, 128], BF16, B0 + 66048)
        self.onesb = at('onesb', [128, 128], BF16, B0 + 66304)
        self.onesrow = at('onesrow_s', [1, 128], BF16, B0 + 66560)
        self.bo_sb = at('bo_sb', [128, 8], F32, B0 + 66816)
        self.gam = at('gam', [128, 8], F32, B0 + 66848)
        self.bet = at('bet', [128, 8], F32, B0 + 66880)
        self.vsync_t = at('vsync_t', [128, 8], F32, B0 + 66912)
        self.epsc = at('epsc', [128, 2], F32, B0 + 66944)
        self.clipp = at('clipp', [128, 512], F32, B0 + 67072)
        self.C0 = B0 + 69120
        self.xn1_own = at('xn1_own', [128, NKC * T_OWN], BF16, self.C0)
        self.D0 = 17504 + 2048 + 99840
        self.xn1_part = at('xn1_part', [128, NKC * T_OWN], BF16, self.D0)
        self.E0 = 17504 + 2048 + 132608
        self.F0 = 17504 + 2048 + 165376
        self.G0 = 17504 + 2048 + 181760
        S = 17504 + 2048 + 198144
        self.kl = at('kl', [64, H * M], BF16, S); S += 2048
        self.Wh = at('Wh', [64, H * HS], BF16, S); S += 2048
        self.xmean8 = at('xmean8', [128, NKC * M], F32, S); S += 2048
        self.xmeanb = at('xmeanb', [128, NKC * M], BF16, S); S += 1024
        self.rec16 = at('rec16', [64, 16], F32, S); S += 64
        self.st16 = at('st16', [64, 16], F32, S); S += 64
        self.st2 = at('st2', [64, 16], F32, S); S += 64
        self.recv = at('recv', [64, 16], F32, S); S += 64
        self.colr = at('colr', [16, 4], F32, S); S += 32
        self.scrow = at('scrow', [1, 16], BF16, S); S += 32
        self.scol = at('scol', [64, 16], F32, S); S += 64
        assert S <= 229344, ('SBUF overflow', S)

        self.pb = [ctx.enter_context(nc.psum_tensor(f'pb{i}', [128, 512],
                                                    F32)) for i in range(8)]
        self.pbb = [t.bitcast(BF16) for t in self.pb]

        SS = lambda n: Sem(ctx, nc, n)
        self.sX = SS('sX'); self.sW = SS('sW')
        self.sPE = SS('sPE'); self.sVE = SS('sVE'); self.sSC = SS('sSC')
        self.sST = SS('sST')
        self.sGU = SS('sGU'); self.sDN = SS('sDN'); self.sGP = SS('sGP')

        for (dst, srcd) in [(self.idf, self.d_id128f),
                            (self.idb, self.d_id128b),
                            (self.onesb, self.d_ones),
                            (self.onesrow, self.d_onesrow)]:
            self.y(lambda e, dst=dst, srcd=srcd: e.dma_start(
                dst[:, :], srcd.ap()[:, :]), inc=self.sX)
        self.y(lambda e: e.dma_start(
            self.bo_sb[:, :], self.bo_d.rearrange('(c p) -> p c', p=128),
            allow_slow_non_contiguous=True),
            inc=self.sX)
        self.y(lambda e: e.dma_start(
            self.gam[:, :], self.gamma_d.rearrange('(c p) -> p c', p=128),
            allow_slow_non_contiguous=True),
            inc=self.sX)
        self.y(lambda e: e.dma_start(
            self.bet[:, :], self.beta_d.rearrange('(c p) -> p c', p=128),
            allow_slow_non_contiguous=True),
            inc=self.sX)
        self.y(lambda e, d_epsc=d_epsc: e.dma_start(self.epsc[:, :], d_epsc.ap()[:, :]),
               inc=self.sX)
        self.y(lambda e, d_clipp=d_clipp: e.dma_start(
            self.clipp[:, :], d_clipp.ap()[:, :]), inc=self.sX)
        self.onesf = self.at('onesf_sb', [128, 128], F32, 228800)
        self.y(lambda e: e.dma_start(self.onesf[:, :],
                                     self.d_onesf.ap()[:, :]), inc=self.sX)
        self.const_ready = self.sX.n
        self.v(lambda e: e.memset(self.vsync_t[:, :], 0.0))

        phases = [self.ph1, self.ph2, self.ph3, self.ph4, self.ph5, self.ph6,
                  self.ph7]
        for i, ph in enumerate(phases):
            if self.upto >= i + 1:
                ph(sb)

        self.finish()
        return self.nc

    def dump(self, name, ap_sb, shape, dtp=F32):
        if name in self.dbg:
            dd = self.nc.dram_tensor('dbg_' + name, list(shape), dtp,
                                     kind="ExternalOutput").ap()
            self.dumps.append((dd, ap_sb))

    def dump_now(self, name, ap_sb, shape, dtp=F32):
        if name in self.dbg:
            dd = self.nc.dram_tensor('dbg_' + name, list(shape), dtp,
                                     kind="ExternalOutput").ap()
            self.y(lambda e, dd=dd, ap_sb=ap_sb: e.dma_start(dd, ap_sb),
                   waits=[(self.sVE, self.sVE.n), (self.sSC, self.sSC.n),
                          (self.sPE, self.sPE.n)],
                   inc=self.sST)
            for eng in ('vector', 'scalar', 'tensor', 'gpsimd'):
                self.wait(eng, self.sST)

    def finish(self):
        for dd, src in self.dumps:
            self.y(lambda e, dd=dd, src=src: e.dma_start(dd, src),
                   waits=[(self.sVE, self.sVE.n), (self.sSC, self.sSC.n),
                          (self.sPE, self.sPE.n)],
                   inc=self.sST)
        self.wait('sync', self.sST)
        with self.nc.Block() as block:
            for eng in ("sync", "gpsimd", "tensor", "vector", "scalar"):
                fns = self.ops[eng]
                getattr(block, eng)(lambda e, fns=fns: [f(e) for f in fns])

    def ph1(self, sb):
        sX, sPE, sVE, sSC = self.sX, self.sPE, self.sVE, self.sSC
        pb = self.pb
        E0 = self.E0; at = self.at
        xtm = [at(f'xtm{i}', [128, D], F32, E0 + 4096 * i) for i in range(2)]
        xfh = at('xfh', [128, NKC * 512], BF16, E0 + 8192)
        sqb = [at(f'sqb{i}', [128, 512], BF16, E0 + 16384 + 1024 * i)
               for i in range(2)]
        mb = at('mb', [128, 512], F32, E0 + 18432)
        rb = at('rb', [128, 512], F32, E0 + 20480)
        tmp = at('tmp_p1', [128, 512], F32, E0 + 22528)
        tmp2 = at('tmp2_p1', [128, 512], F32, E0 + 24576)

        xfh2 = at('xfh2', [128, NKC * 512], BF16, self.G0)
        xfhs = [xfh, xfh2]
        evac_v = None
        evac_s = None
        pe_cch = {}
        stats_free = 0
        slab_info = {}

        def load(slab):
            own = slab < 4
            src = self.x_own if own else self.x_part
            xf = xfhs[slab % 2]
            t0 = (slab % 4) * 512
            nonlocal evac_v, evac_s
            for cch in range(4):
                buf = xtm[cch % 2]
                w = []
                if (slab, cch - 2) in pe_cch or (slab - 1, cch + 2) in pe_cch:
                    key = (slab, cch - 2) if cch >= 2 else (slab - 1, cch + 2)
                    w = [(sPE, pe_cch[key])]
                self.y(lambda e, buf=buf, src=src, t0=t0, cch=cch:
                       e.dma_start(buf[:, :],
                                   src[t0 + 128 * cch:t0 + 128 * (cch + 1),
                                       :]),
                       waits=w, inc=sX)
                ld = sX.n
                for kc in range(NKC):
                    bank = pb[2 + kc // 4]
                    sl = kc % 4
                    w = []
                    if kc == 0:
                        w.append((sX, ld))
                        if evac_v is not None:
                            w.append(evac_v)
                        if slab == 0 and cch == 0:
                            w.append((sX, self.const_ready))
                    if kc == 4 and evac_s is not None:
                        w.append(evac_s)
                    self.t(lambda e, bank=bank, sl=sl, buf=buf, kc=kc:
                           e.transpose(bank[:, 128 * sl:128 * (sl + 1)],
                                       buf[:, 128 * kc:128 * (kc + 1)],
                                       self.idf[:, :]),
                           waits=w, inc=sPE)
                    if kc == 3:
                        pe_half = sPE.n
                pe_cch[(slab, cch)] = sPE.n
                for half in range(2):
                    bank = pb[2 + half]
                    if own:
                        dst = self.resid[:, :].rearrange(
                            'p (kc t) -> p kc t', kc=NKC)[
                            :, 4 * half:4 * half + 4,
                            t0 + 128 * cch:t0 + 128 * cch + 128]
                    else:
                        dst = xf[:, :].rearrange(
                            'p (kc t) -> p kc t', kc=NKC)[
                            :, 4 * half:4 * half + 4,
                            128 * cch:128 * cch + 128]
                    srcp = bank[:, :].rearrange('p (k c) -> p k c', k=4)
                    if half == 0:
                        self.v(lambda e, dst=dst, srcp=srcp: e.tensor_copy(
                            dst, srcp), waits=[(sPE, pe_half)], inc=sVE)
                        evac_v = (sVE, sVE.n)
                    else:
                        self.s(lambda e, dst=dst, srcp=srcp: e.copy(
                            dst, srcp), waits=[(sPE, sPE.n)], inc=sSC)
                        evac_s = (sSC, sSC.n)
            slab_info[slab] = dict(own=own, t0=t0, xf=xf,
                                   ve=sVE.n, sc=sSC.n)

        def stats(slab):
            nonlocal stats_free
            si = slab_info[slab]
            own, t0, xf = si['own'], si['t0'], si['xf']
            ve_done, sc_done = si['ve'], si['sc']
            sq_prod = []
            for kc in range(NKC):
                q = sqb[kc % 2]
                if own:
                    hsrc = self.resid[:, kc * T_OWN + t0:
                                      kc * T_OWN + t0 + 512]
                else:
                    hsrc = xf[:, 512 * kc:512 * (kc + 1)]
                w = [(sVE, ve_done)] if kc == 0 else []
                if kc >= 2:
                    w.append((sPE, sq_prod[kc - 2]))
                self.s(lambda e, q=q, hsrc=hsrc: e.activation(
                    q[:, :], hsrc, AF.Square),
                    waits=w, inc=sSC)
                sq_sc = sSC.n
                self.t(lambda e, kc=kc, hsrc=hsrc, own=own: e.matmul(
                    pb[0][:, :],
                    (self.onesf if own else self.onesb)[:, :], hsrc,
                    start=(kc == 0), stop=(kc == NKC - 1)),
                    waits=([(sVE, ve_done), (sSC, sc_done),
                            (sVE, stats_free)] if kc == 0 else []))
                self.t(lambda e, q=q, kc=kc: e.matmul(
                    pb[1][:, :], self.onesb[:, :], q[:, :],
                    start=(kc == 0), stop=(kc == NKC - 1)),
                    waits=[(sSC, sq_sc)], inc=sPE)
                sq_prod.append(sPE.n)
            self.v(lambda e: e.tensor_scalar_mul(mb[:, :], pb[0][:, :],
                                                 1.0 / D),
                   waits=[(sPE, sPE.n)])
            self.v(lambda e: e.tensor_mul(tmp[:, :], mb[:, :], mb[:, :]),
                   inc=sVE)
            self.s(lambda e: e.add(tmp[:, :], tmp[:, :], self.epsc[:, 1:2]),
                   waits=[(sVE, sVE.n)], inc=sSC)
            self.v(lambda e: e.scalar_tensor_tensor(
                rb[:, :], pb[1][:, :], 1.0 / D, tmp[:, :],
                ALU.mult, ALU.subtract), waits=[(sSC, sSC.n)],
                inc=sVE)
            stats_free = sVE.n
            self.s(lambda e: e.activation(tmp[:, :], rb[:, :], AF.Ln),
                   waits=[(sVE, sVE.n)])
            self.s(lambda e: e.activation(rb[:, :], tmp[:, :], AF.Exp,
                                          scale=-0.5), inc=sSC)
            si['stat_done'] = sSC.n

        def apply(slab):
            si = slab_info[slab]
            own, t0, xf = si['own'], si['t0'], si['xf']
            dstx = self.xn1_own if own else self.xn1_part
            for kc in range(NKC):
                if own:
                    xin = self.resid[:, kc * T_OWN + t0: kc * T_OWN + t0 + 512]
                else:
                    xin = xf[:, 512 * kc:512 * (kc + 1)]
                dcol = kc * T_OWN + t0
                self.v(lambda e, xin=xin: e.tensor_sub(tmp2[:, :], xin,
                                                       mb[:, :]),
                       waits=([(sSC, si['stat_done'])] if kc == 0 else []))
                self.v(lambda e, dstx=dstx, dcol=dcol: e.tensor_mul(
                    dstx[:, dcol:dcol + 512], tmp2[:, :], rb[:, :]))
                self.v(lambda e, dstx=dstx, dcol=dcol, kc=kc, slab=slab:
                       e.tensor_reduce(
                           self.xmean8[:, kc * M + 8 * slab:
                                       kc * M + 8 * slab + 8],
                           dstx[:, dcol:dcol + 512].rearrange(
                               'p (g k) -> p g k', g=8),
                           AX.X, ALU.add), inc=sVE)

        load(0)
        for slab in range(8):
            stats(slab)
            if slab + 1 < 8:
                load(slab + 1)
            apply(slab)
        self.vsync()
        sv_xm = sVE.n
        for kc in range(NKC):
            self.s(lambda e, kc=kc: e.activation(
                self.xmeanb[:, kc * M:(kc + 1) * M],
                self.xmean8[:, kc * M:(kc + 1) * M], AF.Copy,
                scale=1.0 / 64),
                waits=([(sVE, sv_xm)] if kc == 0 else []),
                inc=(sSC if kc == NKC - 1 else None))
        self.wait('vector', sSC)
        self.v(lambda e: e.tensor_copy(self.vsync_t[:, 0:4],
                                       self.vsync_t[:, 4:8]), inc=sVE)
        self.p1_done_v = sVE.n
        self.dump('xn1_own', self.xn1_own[:, :], [128, NKC * T_OWN], BF16)
        self.dump('xn1_part', self.xn1_part[:, :], [128, NKC * T_OWN], BF16)
        self.dump('xmean', self.xmeanb[:, :], [128, NKC * M], BF16)
        self.dump('resid1', self.resid[:, :], [128, NKC * T_OWN], F32)

    def ph2(self, sb):
        sX, sW, sPE, sVE, sSC = self.sX, self.sW, self.sPE, self.sVE, self.sSC
        pb = self.pb
        E0 = self.E0; at = self.at
        wq = at('wq_sb', [128, NKC * D], BF16, self.F0)
        wkh = at('wk_half', [128, 4 * D], BF16, E0 + 0)
        wkt = [at(f'wkt{i}', [64, D], BF16, E0 + 8192 + 2048 * i)
               for i in range(2)]
        self.i7 = at('i7', [64, 1024], BF16, E0 + 12288)
        self.i15 = at('i15', [64, 1024], BF16, E0 + 14336)
        self.i13 = at('i13', [64, 1024], BF16, E0 + 16384)
        self.zTA = at('zTA', [64, H * M], F32, E0 + 0)
        self.s2n = at('s2n', [64, H * M], F32, E0 + 4096)
        self.zA = at('zA', [64, H * M], F32, E0 + 4096)
        self.xs1 = at('xs1', [64, H * M], F32, E0 + 18432)
        self.xs2 = at('xs2', [64, H * M], F32, E0 + 22528)
        self.s2nT = at('s2nT', [64, H * M], F32, E0 + 26624)
        self.ql = at('ql', [64, H * M], BF16, E0 + 30720)
        self.wq = wq
        st2 = self.st2; colr = self.colr; scrow = self.scrow
        scol = self.scol
        for (dst, srcd) in [(self.i7, self.d_i7), (self.i15, self.d_i15),
                            (self.i13, self.d_i13)]:
            self.y(lambda e, dst=dst, srcd=srcd: e.dma_start(
                dst[:, :], srcd.ap()[:, :]),
                waits=[(sVE, self.p1_done_v)], inc=sX)
        ipack_ready = sX.n
        for kc in range(NKC):
            self.g(lambda e, kc=kc: e.dma_start(
                wq[:, D * kc:D * (kc + 1)],
                self.wq_d[128 * kc:128 * (kc + 1), :]), inc=sW)
        w_ready = sW.n
        self.wq_ready = w_ready
        vstart = self.p1_done_v

        for h in range(H):
            bank = pb[4 + h // 8]
            off = 64 * (h % 8)
            for kc in range(NKC):
                w = []
                if h == 0 and kc == 0:
                    w = [(sW, w_ready), (sVE, vstart)]
                self.t(lambda e, h=h, kc=kc, bank=bank, off=off:
                       e.matmul(bank[:64, off:off + 64],
                                wq[:, D * kc + 64 * h:
                                   D * kc + 64 * h + 64],
                                self.xmeanb[:, kc * M:(kc + 1) * M],
                                start=(kc == 0),
                                stop=(kc == NKC - 1)),
                       waits=w, inc=(sPE if kc == NKC - 1 else None))
        for bi in range(2):
            self.s(lambda e, bi=bi: e.activation(
                self.ql[:, 512 * bi:512 * (bi + 1)], pb[4 + bi][:64, :],
                AF.Copy, scale=0.125),
                waits=[(sPE, sPE.n)], inc=sSC)
        ql_evac = sSC.n
        klh = [self.xs1, self.xs2]
        for half in range(2):
            self.g(lambda e, half=half: e.dma_start(
                wkh[:, :].rearrange('p (c f) -> p c f', c=4),
                self.wk_d[512 * half:512 * (half + 1), :].rearrange(
                    '(c p) f -> p c f', p=128)),
                waits=[(sPE, sPE.n)], inc=sW)
            wk_ready = sW.n
            for h in range(H):
                bank = pb[4 + h // 8]
                off = 64 * (h % 8)
                for kc4 in range(4):
                    kc = 4 * half + kc4
                    w = []
                    if kc4 == 0 and h == 0:
                        w = [(sW, wk_ready)]
                        w.append((sSC, sSC.n))
                    self.t(lambda e, h=h, kc=kc, kc4=kc4, bank=bank, off=off:
                           e.matmul(bank[:64, off:off + 64],
                                    wkh[:, D * kc4 + 64 * h:
                                        D * kc4 + 64 * h + 64],
                                    self.xmeanb[:, kc * M:(kc + 1) * M],
                                    start=(kc4 == 0), stop=(kc4 == 3)),
                           waits=w, inc=(sPE if (kc4 == 3 and h == H - 1)
                                         else None))
            for bi in range(2):
                if half == 0:
                    self.s(lambda e, bi=bi, half=half: e.activation(
                        klh[bi][:, :].rearrange('p (a) -> p a') if False else
                        klh[bi][:, 0:512], pb[4 + bi][:64, :],
                        AF.Copy, scale=1.0),
                        waits=[(sPE, sPE.n)], inc=sSC)
                else:
                    self.v(lambda e, bi=bi: e.tensor_add(
                        self.kl[:, 512 * bi:512 * (bi + 1)],
                        klh[bi][:, 0:512], pb[4 + bi][:64, :]),
                        waits=[(sPE, sPE.n)], inc=sVE)
        self.v(lambda e: e.engine_nop(), inc=sVE)
        qlkl_v = sVE.n
        qlkl = sSC.n

        for h in range(H):
            bank = pb[4 + h // 8]; off = 64 * (h % 8)
            self.t(lambda e, h=h, bank=bank, off=off: e.matmul(
                bank[:64, off:off + 64],
                self.ql[:, 64 * h:64 * h + 64],
                self.kl[:, 64 * h:64 * h + 64], start=True, stop=True),
                waits=([(sSC, qlkl), (sVE, qlkl_v)] if h == 0 else []),
                inc=(sPE if h == H - 1 else None))
        for bi in range(2):
            self.s(lambda e, bi=bi: e.activation(
                self.xs1[:, 512 * bi:512 * (bi + 1)], pb[4 + bi][:64, :],
                AF.Exp), waits=[(sPE, sPE.n)], inc=sSC)
        self.dump_now('xs1D', self.xs1[:, :], [64, H * M], F32)
        self.v(lambda e: e.tensor_reduce(
            self.st16[:, :], self.xs1[:, :].rearrange('p (h m) -> p h m', h=H),
            AX.X, ALU.add), waits=[(sSC, sSC.n)])
        self.vsync()
        self.v(lambda e: e.reciprocal(self.rec16[:, :], self.st16[:, :]),
               inc=sVE)
        self.vsync()
        sv_rec = self.sVE.n
        for h in range(H):
            self.s(lambda e, h=h: e.mul(
                self.s2n[:, 64 * h:64 * h + 64],
                self.xs1[:, 64 * h:64 * h + 64],
                self.rec16[:, h:h + 1]),
                waits=([(self.sVE, sv_rec)] if h == 0 else []),
                inc=(self.sSC if h == H - 1 else None))
        self.wait('vector', self.sSC)
        self.dump_now('s2nD', self.s2n[:, :], [64, H * M], F32)
        self.dump_now('rec16D', self.rec16[:, :], [64, 16], F32)
        self.v(lambda e: e.tensor_mul(self.st16[:, :], self.st16[:, :],
                                      self.rec16[:, :]), inc=sVE)
        s2n_done = sVE.n
        for h in range(H):
            bank = pb[4 + h // 8]; off = 64 * (h % 8)
            self.t(lambda e, h=h, bank=bank, off=off: e.matmul(
                bank[:64, off:off + 64], self.s2n[:, 64 * h:64 * h + 64],
                self.idf[:64, :64], start=True, stop=True),
                waits=([(sVE, s2n_done)] if h == 0 else []),
                inc=(sPE if h == H - 1 else None))
        for bi in range(2):
            self.v(lambda e, bi=bi: e.tensor_copy(
                self.s2nT[:, 512 * bi:512 * (bi + 1)], pb[4 + bi][:64, :]),
                waits=[(sPE, sPE.n)], inc=sVE)
        s2nT_done = sVE.n
        self.dump_now('s2nD2', self.s2n[:, :], [64, H * M], F32)

        st2 = self.st2
        self.v(lambda e: e.tensor_reduce(
            st2[:, :], self.s2nT[:, :].rearrange('p (h m) -> p h m', h=H),
            AX.X, ALU.add), waits=[(sVE, s2nT_done)], inc=sVE)
        self.t(lambda e: e.matmul(pb[6][:16, 0:64], self.st16[:, :],
                                  self.idf[:64, :64], start=True, stop=True),
               waits=[(sVE, sVE.n)])
        self.t(lambda e: e.matmul(pb[6][:16, 64:128], st2[:, :],
                                  self.idf[:64, :64], start=True, stop=True),
               inc=sPE)

        self.v(lambda e: e.tensor_reduce(colr[:, 0:1], pb[6][:16, 0:64],
                                         AX.X, ALU.max),
               waits=[(sPE, sPE.n)])
        self.v(lambda e: e.tensor_reduce(colr[:, 1:2], pb[6][:16, 64:128],
                                         AX.X, ALU.max))
        self.vsync()
        self.v(lambda e: e.tensor_mul(colr[:, 2:3], colr[:, 0:1],
                                      colr[:, 1:2]))
        self.vsync()
        self.v(lambda e: e.reciprocal(colr[:, 3:4], colr[:, 2:3]), inc=sVE)
        self.t(lambda e: e.transpose(pb[6][:1, 0:16], colr[:, 3:4],
                                     self.idf[:16, :16]),
               waits=[(sVE, sVE.n)], inc=sPE)
        self.v(lambda e: e.tensor_copy(scrow[:, :], pb[6][:1, 0:16]),
               waits=[(sPE, sPE.n)], inc=sVE)
        self.t(lambda e: e.matmul(pb[6][:64, 32:48], self.onesrow[:1, 0:64],
                                  scrow[:1, :], start=True, stop=True),
               waits=[(sVE, sVE.n)], inc=sPE)
        self.v(lambda e: e.tensor_copy(scol[:, :], pb[6][:64, 32:48]),
               waits=[(sPE, sPE.n)], inc=sVE)
        self.vsync()
        self.dump_now('scolD', scol[:, :], [64, 16], F32)
        self.dump_now('s2nTD', self.s2nT[:, :], [64, H * M], F32)
        self.dump_now('st16D', self.st16[:, :], [64, 16], F32)
        self.dump_now('colrD', colr[:, :], [16, 4], F32)
        sv_sc = sVE.n
        for h in range(H):
            self.s(lambda e, h=h: e.mul(
                self.zA[:, 64 * h:64 * h + 64],
                self.s2nT[:, 64 * h:64 * h + 64], scol[:, h:h + 1]),
                waits=([(sVE, sv_sc)] if h == 0 else []),
                inc=(sSC if h == H - 1 else None))
        self.wait('vector', sSC)
        self.v(lambda e: e.tensor_copy(self.vsync_t[:, 0:4],
                                       self.vsync_t[:, 4:8]), inc=sVE)
        self.dump_now('z0D', self.zA[:, :], [64, H * M], F32)
        for h in range(H):
            bank = pb[4 + h // 8]; off = 64 * (h % 8)
            self.t(lambda e, h=h, bank=bank, off=off: e.matmul(
                bank[:64, off:off + 64], self.zA[:, 64 * h:64 * h + 64],
                self.idf[:64, :64], start=True, stop=True),
                waits=([(sVE, sVE.n)] if h == 0 else []),
                inc=(sPE if h == H - 1 else None))
        for bi in range(2):
            self.v(lambda e, bi=bi: e.tensor_copy(
                self.zTA[:, 512 * bi:512 * (bi + 1)], pb[4 + bi][:64, :]),
                waits=[(sPE, sPE.n)], inc=sVE)

        Z, ZT = self.zA, self.zTA
        for it in range(6):
            for h in range(H):
                b1 = pb[4 + h // 8]; b2 = pb[6 + h // 8]
                off = 64 * (h % 8)
                self.t(lambda e, h=h, b1=b1, off=off, Z=Z: e.matmul(
                    b1[:64, off:off + 64], self.s2nT[:, 64 * h:64 * h + 64],
                    Z[:, 64 * h:64 * h + 64], start=True, stop=True),
                    waits=([(sVE, sVE.n)] if h == 0 else []))
                self.t(lambda e, h=h, b2=b2, off=off, Z=Z: e.matmul(
                    b2[:64, off:off + 64], Z[:, 64 * h:64 * h + 64],
                    self.s2nT[:, 64 * h:64 * h + 64], start=True, stop=True),
                    inc=(sPE if h == H - 1 else None))
            for bi in range(2):
                self.v(lambda e, bi=bi: e.tensor_sub(
                    self.xs2[:, 512 * bi:512 * (bi + 1)],
                    self.i7[:, 512 * bi:512 * (bi + 1)], pb[4 + bi][:64, :]),
                    waits=[(sPE, sPE.n)] + ([(sX, ipack_ready)]
                                            if (it == 0 and bi == 0) else []))
                self.v(lambda e, bi=bi: e.tensor_copy(
                    self.xs1[:, 512 * bi:512 * (bi + 1)], pb[6 + bi][:64, :]),
                    inc=(sVE if bi == 1 else None))
            if it == 0:
                self.dump_now('xt0D', self.xs1[:, :], [64, H * M], F32)
                self.dump_now('b0D', self.xs2[:, :], [64, H * M], F32)
            for h in range(H):
                b1 = pb[4 + h // 8]; off = 64 * (h % 8)
                self.t(lambda e, h=h, b1=b1, off=off: e.matmul(
                    b1[:64, off:off + 64], self.xs1[:, 64 * h:64 * h + 64],
                    self.xs2[:, 64 * h:64 * h + 64], start=True, stop=True),
                    waits=([(sVE, sVE.n)] if h == 0 else []),
                    inc=(sPE if h == H - 1 else None))
            if it == 0:
                self.dump_now('a2pre', self.xs2[:, :], [64, H * M], F32)
            for bi in range(2):
                self.v(lambda e, bi=bi: e.tensor_sub(
                    self.xs2[:, 512 * bi:512 * (bi + 1)],
                    self.i15[:, 512 * bi:512 * (bi + 1)], pb[4 + bi][:64, :]),
                    waits=[(sPE, sPE.n)], inc=(sVE if bi == 1 else None))
            if it == 0:
                self.dump_now('a2D', self.xs2[:, :], [64, H * M], F32)
            for h in range(H):
                b1 = pb[4 + h // 8]; off = 64 * (h % 8)
                self.t(lambda e, h=h, b1=b1, off=off: e.matmul(
                    b1[:64, off:off + 64], self.xs1[:, 64 * h:64 * h + 64],
                    self.xs2[:, 64 * h:64 * h + 64], start=True, stop=True),
                    waits=([(sVE, sVE.n)] if h == 0 else []),
                    inc=(sPE if h == H - 1 else None))
            for bi in range(2):
                self.v(lambda e, bi=bi: e.tensor_sub(
                    self.xs2[:, 512 * bi:512 * (bi + 1)],
                    self.i13[:, 512 * bi:512 * (bi + 1)], pb[4 + bi][:64, :]),
                    waits=[(sPE, sPE.n)], inc=(sVE if bi == 1 else None))
            if it == 0:
                self.dump_now('mD', self.xs2[:, :], [64, H * M], F32)
            for h in range(H):
                b1 = pb[4 + h // 8]; b2 = pb[6 + h // 8]; off = 64 * (h % 8)
                self.t(lambda e, h=h, b1=b1, off=off, ZT=ZT: e.matmul(
                    b1[:64, off:off + 64], ZT[:, 64 * h:64 * h + 64],
                    self.xs2[:, 64 * h:64 * h + 64], start=True, stop=True),
                    waits=([(sVE, sVE.n)] if h == 0 else []))
                self.t(lambda e, h=h, b2=b2, off=off, ZT=ZT: e.matmul(
                    b2[:64, off:off + 64], self.xs2[:, 64 * h:64 * h + 64],
                    ZT[:, 64 * h:64 * h + 64], start=True, stop=True),
                    inc=(sPE if h == H - 1 else None))
            for bi in range(2):
                self.v(lambda e, bi=bi, Z=Z: e.tensor_scalar_mul(
                    Z[:, 512 * bi:512 * (bi + 1)], pb[4 + bi][:64, :], 0.25),
                    waits=[(sPE, sPE.n)])
                self.v(lambda e, bi=bi, ZT=ZT: e.tensor_scalar_mul(
                    ZT[:, 512 * bi:512 * (bi + 1)], pb[6 + bi][:64, :],
                    0.25), inc=(sVE if bi == 1 else None))
            self.dump_now(f'z{it + 1}D', Z[:, :], [64, H * M], F32)
        self.Zfin, self.ZTfin = Z, ZT
        self.pinv_done = sVE.n
        self.dump('ql', self.ql[:, :], [64, H * M], BF16)
        self.dump('kl', self.kl[:, :], [64, H * M], BF16)
        self.dump('s2n', self.s2n[:, :], [64, H * M], F32)
        self.dump('zfin', Z[:, :], [64, H * M], F32)
        self.dump('ztfin', ZT[:, :], [64, H * M], F32)

        self.WkQ = at('WkQ', [128, NKC * 1024], BF16, self.G0)
        wkt_free = [0, 0]
        for h in range(H):
            self.g(lambda e, h=h, b=wkt[h % 2]: e.dma_start(
                b[:, :], self.wkT_d[h, :, :]),
                waits=[(sVE, wkt_free[h % 2])], inc=sW)
            wl = sW.n
            bank = pb[h % 2]
            for kc in range(NKC):
                w = [(sW, wl)]
                if kc == 0:
                    w.append((sVE, sVE.n))
                    if h == 0:
                        w.append((sSC, qlkl))
                self.t(lambda e, h=h, kc=kc, bank=bank, b=wkt[h % 2]:
                       e.matmul(bank[:, 64 * kc:64 * kc + 64],
                                b[:, 128 * kc:128 * (kc + 1)],
                                self.ql[:, 64 * h:64 * h + 64],
                                start=True, stop=True),
                       waits=w, inc=(sPE if kc == NKC - 1 else None))
            self.v(lambda e, h=h, bank=bank: e.tensor_copy(
                self.WkQ[:, :].rearrange('p (kc hm) -> p kc hm', kc=NKC)
                [:, :, 64 * h:64 * h + 64],
                bank[:, :].rearrange('p (kc m) -> p kc m', kc=NKC)),
                waits=[(sPE, sPE.n)], inc=sVE)
            wkt_free[h % 2] = sVE.n
        self.wkq_done = sVE.n
        self.dump('WkQ', self.WkQ[:, :], [128, NKC * 1024], BF16)

    def ph3(self, sb):
        sX, sW, sPE, sVE, sSC = self.sX, self.sW, self.sPE, self.sVE, self.sSC
        pb = self.pb
        E0 = self.E0; at = self.at
        wv = at('wv_sb', [128, NKC * D], BF16, E0 + 4096)
        vaug = [at(f'vaug{i}', [128, H * 65], BF16, E0 + 20480 + 2080 * i)
                for i in range(2)]
        ptm = [at(f'ptm{i}', [128, H * M], BF16, E0 + 24640 + 2048 * i)
               for i in range(2)]
        Ef = at('E_f32', [64, H * HS], F32, self.G0)
        for kc in range(NKC):
            self.g(lambda e, kc=kc: e.dma_start(
                wv[:, D * kc:D * (kc + 1)],
                self.wv_d[128 * kc:128 * (kc + 1), :]),
                waits=([(sPE, sPE.n), (sVE, sVE.n)] if kc == 0 else []),
                inc=sW)
        wv_ready = self.sW.n
        self.g(lambda e: e.memset(
            vaug[0][:, :].rearrange('p (h c) -> p h c', h=H)[:, :, 64:65],
            1.0), k=1, inc=None)
        self.g(lambda e: e.memset(
            vaug[1][:, :].rearrange('p (h c) -> p h c', h=H)[:, :, 64:65],
            1.0), k=1, inc=self.sGP)
        vaug_ones = self.sGP.n

        def emit_E(lc, pbuf, vb, p_done, v_done):
            for h in range(H):
                grp = 0 if h < 7 else (1 if h < 14 else 2)
                ebank = pb[4 + grp]
                eo = ecol(h) % 512
                w = [(sSC, p_done)]
                if h == 0:
                    w.append((sVE, v_done))
                self.t(lambda e, h=h, ebank=ebank, eo=eo, pbuf=pbuf, vb=vb,
                       lc=lc: e.matmul(
                           ebank[:64, eo:eo + 65],
                           pbuf[:, 64 * h:64 * h + 64],
                           vb[:, 65 * h:65 * h + 65],
                           start=(lc == 0 and h in (0, 7, 14)),
                           stop=(lc == 31 and h in (6, 13, 15))),
                       waits=w, inc=(sPE if h == H - 1 else None))

        pendE = None
        vban_free = 0
        sban_free = 0
        for lc in range(32):
            part = lc >= 16
            xn = self.xn1_part if part else self.xn1_own
            t0 = (lc % 16) * 128
            vb = vaug[lc % 2]; pbuf = ptm[lc % 2]
            for jb in range(2):
                bank = pb[jb]
                for kc in range(NKC):
                    w = []
                    if kc == 0:
                        w = [(sVE, vban_free)]
                        if lc == 0 and jb == 0:
                            w += [(sW, wv_ready), (sVE, self.wkq_done),
                                  (self.sGP, vaug_ones)]
                    self.t(lambda e, xn=xn, t0=t0, kc=kc, jb=jb, bank=bank:
                           e.matmul(bank[:, :],
                                    xn[:, kc * T_OWN + t0:
                                       kc * T_OWN + t0 + 128],
                                    wv[:, D * kc + 512 * jb:
                                       D * kc + 512 * (jb + 1)],
                                    start=(kc == 0), stop=(kc == NKC - 1)),
                           waits=w, inc=(sPE if kc == NKC - 1 else None))
                self.v(lambda e, vb=vb, jb=jb, bank=bank: e.tensor_copy(
                    vb[:, :].rearrange('p (h c) -> p h c', h=H)
                    [:, 8 * jb:8 * (jb + 1), 0:64],
                    bank[:, :].rearrange('p (h m) -> p h m', h=8)),
                    waits=[(sPE, sPE.n)], inc=sVE)
            v_done = sVE.n
            vban_free = v_done
            for jb in range(2):
                bank = pb[2 + jb]
                for kc in range(NKC):
                    w = []
                    if kc == 0:
                        w = [(sSC, sban_free)]
                    self.t(lambda e, xn=xn, t0=t0, kc=kc, jb=jb, bank=bank:
                           e.matmul(bank[:, :],
                                    xn[:, kc * T_OWN + t0:
                                       kc * T_OWN + t0 + 128],
                                    self.WkQ[:, 1024 * kc + 512 * jb:
                                             1024 * kc + 512 * (jb + 1)],
                                    start=(kc == 0), stop=(kc == NKC - 1)),
                           waits=w, inc=(sPE if kc == NKC - 1 else None))
                self.s(lambda e, pbuf=pbuf, jb=jb, bank=bank: e.activation(
                    pbuf[:, 512 * jb:512 * (jb + 1)], bank[:, :], AF.Exp),
                    waits=[(sPE, sPE.n)], inc=sSC)
            p_done = sSC.n
            sban_free = p_done
            if pendE is not None:
                emit_E(*pendE)
            pendE = (lc, pbuf, vb, p_done, v_done)
        emit_E(*pendE)
        e_mm_done = self.sPE.n
        recv = self.recv
        for grp, nh in ((0, 7), (1, 7), (2, 2)):
            bank = pb[4 + grp]
            self.v(lambda e, grp=grp, bank=bank, nh=nh: e.reciprocal(
                recv[:, 7 * grp:7 * grp + nh],
                bank[:64, 0:65 * nh].rearrange('p (h c) -> p h c', c=65)
                [:, :, 64:65]),
                waits=([(sPE, e_mm_done)] if grp == 0 else []))
        self.vsync()
        for h in range(H):
            grp = 0 if h < 7 else (1 if h < 14 else 2)
            bank = pb[4 + grp]
            eo = ecol(h) % 512
            ri = 7 * grp + (h - 7 * grp)
            self.v(lambda e, h=h, bank=bank, eo=eo, ri=ri:
                   e.tensor_scalar_mul(
                       Ef[:, 64 * h:64 * h + 64],
                       bank[:64, eo:eo + 64], recv[:, ri:ri + 1]),
                   inc=(sVE if h == H - 1 else None))
        self.vsync()
        E_done = sVE.n
        for h in range(H):
            bank = pb[h % 2]
            self.t(lambda e, h=h, bank=bank: e.matmul(
                bank[:64, 64 * (h // 2):64 * (h // 2) + 64],
                self.ZTfin[:, 64 * h:64 * h + 64],
                Ef[:, 64 * h:64 * h + 64], start=True, stop=True),
                waits=([(sVE, E_done)] if h == 0 else []),
                inc=(sPE if h == H - 1 else None))
        for bi in range(2):
            self.v(lambda e, bi=bi: e.tensor_copy(
                self.Wh[:, :].rearrange('p (g two c) -> p g two c', two=2,
                                        c=64)[:, :, bi, :],
                pb[bi][:64, 0:512].rearrange('p (g c) -> p g c', c=64)),
                waits=[(sPE, sPE.n)], inc=sVE)
        self.vsync()
        self.wh_done = sVE.n
        self.dump('E', Ef[:, :], [64, H * HS], F32)
        self.dump('Wh', self.Wh[:, :], [64, H * HS], BF16)

    def ph4(self, sb):
        sX, sW, sPE, sVE, sSC = self.sX, self.sW, self.sPE, self.sVE, self.sSC
        pb = self.pb
        E0 = self.E0; D0 = self.D0; at = self.at
        wo = at('wo_sb', [128, NKC * D], BF16, self.G0)
        qt = at('qt', [128, 8 * 512], BF16, D0)
        qodd = at('qodd', [64, 8 * 512], BF16, D0 + 8192)
        es1 = at('es1', [64, H * 512], BF16, D0 + 16384)
        mrg = at('mrg', [128, 8 * 512], BF16, E0 + 0)
        sumsb = at('sumsb', [16, 1024], BF16, E0 + 8192)
        r1b = at('r1b', [128, 512], F32, E0 + 9216)
        r1b2 = at('r1b2', [128, 512], F32, E0 + 13824)
        r1b3 = at('r1b3', [128, 512], F32, E0 + 15872)
        self.e16 = at('e16', [64, 256], BF16, E0 + 11264)
        self.b16 = at('b16', [16, 1024], BF16, E0 + 11776)
        self.xn3 = at('xn3', [128, NKC * T_OWN], BF16, self.C0)
        sqb4 = [at(f'sqb4_{i}', [128, 512], BF16, E0 + 17920 + 1024 * i)
                for i in range(4)]
        mb = at('mb_p45', [128, 512], F32, E0 + 22016)
        ra = at('ra_p45', [128, 512], F32, E0 + 24064)
        rbs = at('rb_p45', [128, 512], F32, E0 + 26112)
        tmp = at('tmp_p45', [128, 512], F32, E0 + 28160)
        rbt4 = [r1b, r1b2, r1b3, tmp]
        ug = [at(f'ug_p45{i}', [128, 512], BF16, E0 + 30208 + 1024 * i)
              for i in range(2)]
        gtmp = sqb4[0]
        self.y(lambda e: e.dma_start(self.e16[:, :], self.d_e16.ap()[:, :]),
               waits=[(sPE, sPE.n), (sVE, sVE.n)], inc=sX)
        self.y(lambda e: e.dma_start(self.b16[:, :], self.d_b16.ap()[:, :]),
               inc=sX)
        p4c_ready = sX.n
        for kc in range(NKC):
            self.g(lambda e, kc=kc: e.dma_start(
                wo[:, D * kc:D * (kc + 1)],
                self.wo_d[128 * kc:128 * (kc + 1), :]),
                waits=([(sPE, sPE.n)] if kc == 0 else []), inc=sW)
        wo_ready = self.sW.n

        qev = {0: None, 1: None}
        qev_l = []
        exp_ev = [None] * H
        stt_ev = [None] * NKC
        mm2_done = [0] * 4
        pb4_free = [0] * 4
        pb6_free = [0] * 4
        stat_ready = [0] * 4
        gb_done = [0] * 4
        ug_free = []
        qsplit = None

        def chain(tt):
            self.v(lambda e: e.tensor_scalar_mul(mb[:, :], pb[4][:, :],
                                                 1.0 / D),
                   waits=[(sPE, mm2_done[tt])], inc=sVE)
            pb4_free[tt] = sVE.n
            self.v(lambda e: e.tensor_mul(tmp[:, :], mb[:, :], mb[:, :]),
                   inc=sVE)
            self.s(lambda e: e.add(tmp[:, :], tmp[:, :], self.epsc[:, 1:2]),
                   waits=[(sVE, sVE.n)], inc=sSC)
            self.v(lambda e: e.scalar_tensor_tensor(
                rbs[:, :], pb[6][:, :], 1.0 / D, tmp[:, :],
                ALU.mult, ALU.subtract), waits=[(sSC, sSC.n)],
                inc=sVE)
            pb6_free[tt] = sVE.n
            self.s(lambda e: e.activation(tmp[:, :], rbs[:, :], AF.Ln),
                   waits=[(sVE, sVE.n)])
            self.s(lambda e: e.activation(ra[:, :], tmp[:, :], AF.Exp,
                                          scale=-0.5))
            self.s(lambda e: e.add(tmp[:, :], rbs[:, :], self.epsc[:, 1:2]),
                   inc=sSC)
            self.v(lambda e: e.tensor_mul(tmp[:, :], tmp[:, :], ra[:, :]),
                   waits=[(sSC, sSC.n)])
            self.v(lambda e: e.tensor_mul(tmp[:, :], tmp[:, :], ra[:, :]),
                   inc=sVE)
            self.s(lambda e: e.add(tmp[:, :], tmp[:, :], self.epsc[:, 0:1]),
                   waits=[(sVE, sVE.n)])
            self.s(lambda e: e.activation(tmp[:, :], tmp[:, :], AF.Ln))
            self.s(lambda e: e.activation(tmp[:, :], tmp[:, :], AF.Exp,
                                          scale=-0.5), inc=sSC)
            self.v(lambda e: e.tensor_mul(rbs[:, :], ra[:, :], tmp[:, :]),
                   waits=[(sSC, sSC.n)], inc=sVE)
            stat_ready[tt] = sVE.n

        def apply(tt):
            t0 = 512 * tt
            for kc in range(NKC):
                col = kc * T_OWN + t0
                kp = kc % 2
                ugp = ug[kp]
                w = [(sVE, stat_ready[tt])] if kc == 0 else []
                if len(ug_free) >= 2:
                    w = w + [(sSC, ug_free[-2])]
                self.v(lambda e, col=col: e.tensor_sub(
                    tmp[:, :], self.resid[:, col:col + 512], mb[:, :]),
                    waits=w)
                self.v(lambda e, ugp=ugp: e.tensor_mul(
                    ugp[:, :], tmp[:, :], rbs[:, :]), inc=sVE)
                svv = sVE.n
                self.v(lambda e: e.tensor_mul(
                    tmp[:, :], tmp[:, :], ra[:, :]))
                self.v(lambda e, col=col: e.tensor_add(
                    self.resid[:, col:col + 512],
                    self.resid[:, col:col + 512], tmp[:, :]), inc=sVE)
                ws = [(sVE, svv)]
                if kc == 0:
                    ws.append((sPE, mm2_done[tt]))
                self.s(lambda e, kc=kc, ugp=ugp: e.mul(
                    gtmp[:, :], ugp[:, :], self.gam[:, kc:kc + 1]),
                    waits=ws)
                self.s(lambda e, kc=kc, col=col: e.add(
                    self.xn3[:, col:col + 512], gtmp[:, :],
                    self.bet[:, kc:kc + 1]), inc=sSC)
                ug_free.append(sSC.n)
            gb_done[tt] = sSC.n
            self.apply_v_done = sVE.n

        for tt in range(4):
            t0 = 512 * tt
            if tt > 0:
                chain(tt - 1)
            for pr in range(8):
                bank = pb[pr % 2]
                for kc in range(NKC):
                    w = []
                    if kc == 0:
                        if tt == 0 and pr < 2:
                            w = [(sSC, self.sSC.n), (sVE, self.sVE.n)]
                            if pr == 0:
                                w.append((sVE, self.wh_done))
                                w.append((sW, self.wq_ready))
                        else:
                            w = [qev[pr % 2]]
                    self.t(lambda e, pr=pr, kc=kc, bank=bank, t0=t0:
                           e.matmul(bank[:, :],
                                    self.wq[:, D * kc + 128 * pr:
                                            D * kc + 128 * (pr + 1)],
                                    self.xn1_own[:, kc * T_OWN + t0:
                                                 kc * T_OWN + t0 + 512],
                                    start=(kc == 0), stop=(kc == NKC - 1)),
                           waits=w, inc=(sPE if kc == NKC - 1 else None))
                wv = [(sPE, sPE.n)]
                if pr == 0 and tt > 0:
                    wv.append((sX, qsplit))
                self.v(lambda e, pr=pr, bank=bank: e.tensor_scalar_mul(
                    qt[:, 512 * pr:512 * (pr + 1)], bank[:, :], 0.125),
                    waits=wv, inc=sVE)
                qev[pr % 2] = (sVE, sVE.n)
                qev_l.append((sVE, sVE.n))
            self.y(lambda e: e.dma_start(
                qodd[:, :].rearrange('p (pr c) -> p pr c', pr=8),
                qt[64:128, :].rearrange('p (pr c) -> p pr c', pr=8)),
                waits=[(sVE, sVE.n)], inc=sX)
            qsplit = sX.n
            if tt > 0:
                apply(tt - 1)
            for h in range(H):
                bank = pb[2 + h % 2]
                if h % 2 == 0:
                    qsrc = qt[0:64, 512 * (h // 2):512 * (h // 2) + 512]
                    w = [qev_l[-8 + h // 2]]
                else:
                    qsrc = qodd[:, 512 * (h // 2):512 * (h // 2) + 512]
                    w = [(sX, qsplit)]
                if h >= 2:
                    w.append(exp_ev[h - 2])
                elif tt > 0:
                    w.append(stt_ev[6 + h])
                self.t(lambda e, h=h, bank=bank, qsrc=qsrc: e.matmul(
                    bank[:64, :], self.kl[:, 64 * h:64 * h + 64], qsrc,
                    start=True, stop=True), waits=w,
                    inc=sPE)
                self.s(lambda e, h=h, bank=bank: e.activation(
                    es1[:, 512 * h:512 * (h + 1)], bank[:64, :], AF.Exp),
                    waits=[(sPE, sPE.n)], inc=sSC)
                exp_ev[h] = (sSC, sSC.n)
            for h in range(H):
                w = [exp_ev[h]]
                if h == 0:
                    if tt == 0:
                        w += [(sVE, self.sVE.n), (sX, p4c_ready)]
                    else:
                        w += [(sVE, pb4_free[tt - 1])]
                self.t(lambda e, h=h: e.matmul(
                    pb[4][:16, :], self.e16[:, 16 * h:16 * (h + 1)],
                    es1[:, 512 * h:512 * (h + 1)],
                    start=(h == 0), stop=(h == H - 1)),
                    waits=w, inc=(sPE if h == H - 1 else None))
            self.v(lambda e: e.tensor_copy(sumsb[:16, 0:512], pb[4][:16, :]),
                   waits=[(sPE, sPE.n)], inc=sVE)
            sums_done = (sVE, sVE.n)
            rec_ev = [None, None]
            mult_ev = [None, None]

            def bcast_rec(pr):
                bb = pb[5 + 2 * (pr % 2)]
                w = [sums_done] if pr < 2 else [rec_ev[pr % 2]]
                self.t(lambda e, pr=pr, bb=bb: e.matmul(
                    bb[:, :], self.b16[:, 128 * pr:128 * (pr + 1)],
                    sumsb[:16, 0:512], start=True, stop=True),
                    waits=w, inc=sPE)
                self.v(lambda e, pr=pr, bb=bb: e.reciprocal(
                    rbt4[pr % 4][:, :], bb[:, :]),
                    waits=[(sPE, sPE.n)], inc=sVE)
                rec_ev[pr % 2] = (sVE, sVE.n)

            def outh_mult(pr):
                ob = pb[4 + 2 * (pr % 2)]
                for par in range(2):
                    h = 2 * pr + par
                    w = []
                    if par == 0:
                        if pr < 2:
                            if pr == 0:
                                w = [sums_done]
                            elif tt > 0:
                                w = [(sVE, pb6_free[tt - 1])]
                        else:
                            w = [mult_ev[pr % 2]]
                    self.t(lambda e, h=h, par=par, ob=ob: e.matmul(
                        ob[64 * par:64 * par + 64, :],
                        self.Wh[:, 64 * h:64 * h + 64],
                        es1[:, 512 * h:512 * (h + 1)],
                        start=True, stop=True),
                        waits=w, inc=(sPE if par == 1 else None))
                self.v(lambda e, pr=pr, ob=ob: e.tensor_mul(
                    mrg[:, 512 * pr:512 * (pr + 1)], ob[:, :],
                    rbt4[pr % 4][:, :]),
                    waits=[(sPE, sPE.n)], inc=sVE)
                mult_ev[pr % 2] = (sVE, sVE.n)

            bcast_rec(0)
            bcast_rec(1)
            bcast_rec(2)
            bcast_rec(3)
            outh_mult(0)
            outh_mult(1)
            bcast_rec(4)
            bcast_rec(5)
            outh_mult(2)
            outh_mult(3)
            bcast_rec(6)
            bcast_rec(7)
            outh_mult(4)
            outh_mult(5)
            outh_mult(6)
            outh_mult(7)
            mrg_done = self.sVE.n
            for oc in range(NKC):
                bank = pb[2 + oc % 2]
                for kc in range(NKC):
                    w = []
                    if kc == 0:
                        w = [(sVE, mrg_done)]
                        if oc >= 2:
                            w = [stt_ev[oc - 2]]
                        if tt == 0 and oc == 0:
                            w.append((sW, wo_ready))
                    self.t(lambda e, oc=oc, kc=kc, bank=bank: e.matmul(
                        bank[:, :], wo[:, D * kc + 128 * oc:
                                       D * kc + 128 * (oc + 1)],
                        mrg[:, 512 * kc:512 * (kc + 1)],
                        start=(kc == 0), stop=(kc == NKC - 1)),
                        waits=w, inc=(sPE if kc == NKC - 1 else None))
                self.v(lambda e, oc=oc, t0=t0, bank=bank:
                       e.scalar_tensor_tensor(
                           self.resid[:, oc * T_OWN + t0:
                                      oc * T_OWN + t0 + 512],
                           bank[:, :], self.bo_sb[:, oc:oc + 1],
                           self.resid[:, oc * T_OWN + t0:
                                      oc * T_OWN + t0 + 512],
                           ALU.add, ALU.add),
                       waits=[(sPE, sPE.n)], inc=sVE)
                stt_ev[oc] = (sVE, sVE.n)
            base_pe = sPE.n
            sq_ev = []
            for kc in range(NKC):
                col = kc * T_OWN + t0
                w = []
                if kc >= 4:
                    w.append((sPE, base_pe + (kc - 4) + 1))
                if kc == 0 and tt > 0:
                    w.append((sSC, gb_done[tt - 1]))
                self.v(lambda e, kc=kc, col=col: e.tensor_mul(
                    sqb4[kc % 4][:, :], self.resid[:, col:col + 512],
                    self.resid[:, col:col + 512]),
                    waits=w, inc=sVE)
                sq_ev.append(sVE.n)
            for kc in range(NKC):
                col = kc * T_OWN + t0
                w1 = [(sVE, stt_ev[kc][1])]
                if kc == 0:
                    w1.append((sVE, mult_ev[0][1]))
                self.t(lambda e, kc=kc, col=col: e.matmul(
                    pb[4][:, :], self.onesf[:, :],
                    self.resid[:, col:col + 512],
                    start=(kc == 0), stop=(kc == NKC - 1)), waits=w1)
                w2 = [(sVE, sq_ev[kc])]
                if kc == 0:
                    w2.append((sVE, mult_ev[1][1]))
                self.t(lambda e, kc=kc: e.matmul(
                    pb[6][:, :], self.onesb[:, :], sqb4[kc % 4][:, :],
                    start=(kc == 0), stop=(kc == NKC - 1)), waits=w2,
                    inc=sPE)
            mm2_done[tt] = sPE.n
        chain(3)
        apply(3)
        self.xn3_done = self.apply_v_done
        self.dump('xn3', self.xn3[:, :], [128, NKC * T_OWN], BF16)
        self.dump('acc5', self.resid[:, :], [128, NKC * T_OWN], F32)

    def ph5(self, sb):
        pass

    def ph6(self, sb):
        sX, sW, sPE, sVE, sSC = self.sX, self.sW, self.sPE, self.sVE, self.sSC
        sGU, sDN = self.sGU, self.sDN
        pb = self.pb
        at = self.at; G0 = self.G0; C0 = self.C0
        gub = [at(f'gub{i}', [128, 2048], BF16, G0 + 4096 * i)
               for i in range(3)]
        grow = [at('grow0', [1, 2048], BF16, 215648),
                at('grow1', [1, 2048], BF16, 219744)]
        RB = G0 + 12288
        m2scr = at('m2scr', [128, 16 * 28], F32, RB)
        logit = at('logit', [128, 128], F32, RB + 1824)
        gT = at('gT', [128, 128], F32, 228288)
        rt = at('rt_sb', [128, NKC * NE], BF16, RB + 2848)
        stat = at('stat_p6', [128, 16], F32, RB + 2976)
        stat2 = at('stat2_p6', [128, 16], F32, RB + 3040)
        m2t = at('m2t', [128, 16], F32, RB + 3104)
        gtile = at('gtile', [128, 128], F32, RB + 3168)
        hid = at('hid', [128, 16 * 1024], BF16, self.E0)
        xe = at('xe', [128, NKC * 1024], BF16, self.F0)
        dn = at('dn', [128, 16 * 1024], BF16, self.D0)

        self.g(lambda e: e.dma_start(rt[:, :], self.routerT_d[:, :]),
            waits=[(sPE, sPE.n)], inc=sW)
        rt_ready = sW.n

        for tc in range(16):
            bank = pb[2]
            for kc in range(NKC):
                w = []
                if kc == 0:
                    w = [(sVE, self.xn3_done), (sSC, sSC.n)]
                    if tc == 0:
                        w.append((sW, rt_ready))
                self.t(lambda e, tc=tc, kc=kc, bank=bank: e.matmul(
                    bank[:, 8 * tc:8 * tc + 8],
                    self.xn3[:, kc * T_OWN + 128 * tc:
                             kc * T_OWN + 128 * tc + 128],
                    rt[:, NE * kc:NE * (kc + 1)],
                    start=(kc == 0), stop=(kc == NKC - 1)),
                    waits=w, inc=(sPE if kc == NKC - 1 else None))
            if tc == 7 or tc == 15:
                self.s(lambda e, tc=tc, bank=bank: e.copy(
                    logit[:, 64 * (tc // 8):64 * (tc // 8) + 64],
                    bank[:, 64 * (tc // 8):64 * (tc // 8) + 64]),
                    waits=[(sPE, sPE.n)], inc=sSC)
        lg = sSC.n
        self.v(lambda e: e.tensor_tensor(
            logit[:, :], logit[:, :], self.clipp[:, 0:128], ALU.min),
            waits=[(sSC, lg)])
        self.v(lambda e: e.tensor_tensor(
            logit[:, :], logit[:, :], self.clipp[:, 128:256], ALU.max),
            inc=sVE)
        self.vsync()
        self.v(lambda e: e.tensor_reduce(
            stat[:, :], logit[:, :].rearrange('p (c e) -> p c e', c=16),
            AX.X, ALU.max, negate=True))
        self.vsync()
        for c in range(16):
            self.s(lambda e, c=c: e.activation(
                logit[:, 8 * c:8 * c + 8], logit[:, 8 * c:8 * c + 8],
                AF.Exp, bias=stat[:, c:c + 1]),
                waits=([(sVE, sVE.n)] if c == 0 else []),
                inc=(sSC if c == 15 else None))
        self.v(lambda e: e.tensor_reduce(
            stat2[:, :], logit[:, :].rearrange('p (c e) -> p c e', c=16),
            AX.X, ALU.add), waits=[(sSC, sSC.n)])
        self.vsync()
        self.v(lambda e: e.tensor_scalar_add(stat2[:, :], stat2[:, :], REPS))
        self.vsync()
        self.v(lambda e: e.reciprocal(stat2[:, :], stat2[:, :]), inc=sVE)
        self.vsync()
        sv_r = sVE.n
        for c in range(16):
            self.s(lambda e, c=c: e.mul(
                logit[:, 8 * c:8 * c + 8], logit[:, 8 * c:8 * c + 8],
                stat2[:, c:c + 1]),
                waits=([(sVE, sv_r)] if c == 0 else []),
                inc=(sSC if c == 15 else None))
        self.wait('vector', sSC)
        self.v(lambda e: e.tensor_tensor(
            logit[:, :], logit[:, :], self.clipp[:, 256:384], ALU.min))
        self.v(lambda e: e.tensor_tensor(
            logit[:, :], logit[:, :], self.clipp[:, 384:512], ALU.max),
            inc=sVE)
        self.vsync()
        self.s(lambda e: e.copy(
            gtile[:, :].rearrange('p (e c) -> p c e', e=NE),
            logit[:, :].rearrange('p (c e) -> p c e', c=16)),
            waits=[(sVE, sVE.n)], inc=sSC)
        self.wait('vector', sSC)
        k = 0
        for i in range(NE):
            for j in range(i + 1, NE):
                self.v(lambda e, i=i, j=j, k=k: e.tensor_tensor(
                    m2scr[:, :].rearrange('p (c s) -> p c s', s=28)
                    [:, :, k:k + 1],
                    gtile[:, 16 * i:16 * i + 16],
                    gtile[:, 16 * j:16 * j + 16], ALU.min))
                k += 1
        self.vsync()
        self.v(lambda e: e.tensor_reduce(
            m2t[:, :], m2scr[:, :].rearrange('p (c s) -> p c s', s=28),
            AX.X, ALU.max))
        self.vsync()
        for ei in range(NE):
            self.v(lambda e, ei=ei: e.tensor_tensor(
                logit[:, 16 * ei:16 * (ei + 1)],
                gtile[:, 16 * ei:16 * (ei + 1)], m2t[:, :], ALU.is_ge))
        self.vsync()
        for ei in range(NE):
            self.v(lambda e, ei=ei: e.tensor_tensor(
                gtile[:, 16 * ei:16 * (ei + 1)],
                gtile[:, 16 * ei:16 * (ei + 1)],
                logit[:, 16 * ei:16 * (ei + 1)], ALU.mult),
                inc=(sVE if ei == NE - 1 else None))
        self.vsync()
        self.t(lambda e: e.transpose(pb[3][:, :128], gtile[:, :],
                                     self.idf[:, :]),
               waits=[(sVE, sVE.n)], inc=sPE)
        self.v(lambda e: e.tensor_copy(gT[:, :], pb[3][:, :128]),
               waits=[(sPE, sPE.n)], inc=sVE)
        self.vsync()
        gT_done = sVE.n
        self.dump('gT', gT[:, :], [128, 128], F32)

        sched = [(ei, th, p) for ei in range(NE) for th in range(2)
                 for p in range(16)]
        slice_pe = {}
        slice_dma = {}
        gate_ev = [0, 0]
        up_ev = [0, 0]
        dn_loaded = 0
        xe_done = 0
        grow_ready = 0
        acc_ev = [0, 0, 0, 0]

        def issue_gu(s_idx):
            ei, th, p = sched[s_idx]
            gbuf = gub[s_idx % 3]
            w = []
            if s_idx >= 3:
                w.append((sPE, slice_pe[s_idx - 3]))
            self.g(lambda e, ei=ei, p=p, gbuf=gbuf: e.dma_start(
                gbuf[:, :].rearrange('p (kc j) -> p kc j', kc=NKC),
                self.WG_d[ei, p, :, :].rearrange(
                    'p (kc j) -> p kc j', kc=NKC)),
                waits=w, inc=sGU)
            slice_dma[s_idx] = sGU.n

        for s in range(3):
            issue_gu(s)

        units = [(ei, th) for ei in range(NE) for th in range(2)]
        grow_rdy = {}

        def issue_grow(ei):
            self.g(lambda e, ei=ei: e.dma_start(
                grow[ei % 2][0:1, :], gT[16 * ei:16 * (ei + 1), :]),
                waits=[(sVE, gT_done), (sPE, sPE.n)], inc=sW)
            grow_rdy[ei] = sW.n

        def emit_xe(u):
            ei, th = units[u]
            th0 = 1024 * th
            for ts in range(2):
                tso = th0 + 512 * ts
                self.t(lambda e, tso=tso, ts=ts, ei=ei: e.matmul(
                    pb[6 + ts][:, :], self.onesrow[:1, :],
                    grow[ei % 2][:1, tso:tso + 512], start=True, stop=True),
                    waits=[(sW, grow_rdy[ei]), (sVE, sVE.n)], inc=sPE)
            bc_pe = sPE.n
            for ts in range(2):
                for kc in range(NKC):
                    self.v(lambda e, kc=kc, ts=ts, th0=th0:
                           e.tensor_mul(
                               xe[:, 1024 * kc + 512 * ts:
                                  1024 * kc + 512 * ts + 512],
                               self.xn3[:, kc * T_OWN + th0 + 512 * ts:
                                        kc * T_OWN + th0 + 512 * ts + 512],
                               pb[6 + ts][:, :]),
                           waits=([(sPE, bc_pe)]
                                  if (ts == 0 and kc == 0) else []),
                           inc=(sVE if (ts == 1 and kc == NKC - 1)
                                else None))
            return sVE.n

        issue_grow(0)
        xe_done = emit_xe(0)
        for u, (ei, th) in enumerate(units):
            th0 = 1024 * th
            if th == 0:
                dn_pending = list(range(8))
                dn_wait = (sPE, self.sPE.n)
            if th == 1 and ei + 1 < NE:
                issue_grow(ei + 1)
            for p in range(16):
                s_idx = u * 16 + p
                gbuf = gub[s_idx % 3]
                for ts in range(2):
                    gslot = (2 * s_idx + ts) % 2
                    gbank = pb[0 + 2 * gslot]
                    ubank = pb[1 + 2 * gslot]
                    for kc in range(NKC):
                        w = []
                        if kc == 0:
                            w = [(sGU, slice_dma[s_idx]),
                                 (sVE, xe_done if (p == 0 and ts == 0)
                                  else up_ev[gslot]),
                                 (sSC, gate_ev[gslot])]
                        self.t(lambda e, kc=kc, ts=ts, gbuf=gbuf,
                               gbank=gbank:
                               e.matmul(gbank[:, :],
                                        gbuf[:, 256 * kc:256 * kc + 128],
                                        xe[:, 1024 * kc + 512 * ts:
                                           1024 * kc + 512 * ts + 512],
                                        start=(kc == 0),
                                        stop=(kc == NKC - 1)), waits=w)
                        self.t(lambda e, kc=kc, ts=ts, gbuf=gbuf,
                               ubank=ubank:
                               e.matmul(ubank[:, :],
                                        gbuf[:, 256 * kc + 128:
                                             256 * kc + 256],
                                        xe[:, 1024 * kc + 512 * ts:
                                           1024 * kc + 512 * ts + 512],
                                        start=(kc == 0),
                                        stop=(kc == NKC - 1)),
                               inc=(sPE if kc == NKC - 1 else None))
                    slice_pe[s_idx] = sPE.n
                    self.s(lambda e, p=p, ts=ts, gbank=gbank:
                           e.activation(
                               hid[:, 1024 * p + 512 * ts:
                                   1024 * p + 512 * ts + 512],
                               gbank[:, :], AF.Silu),
                           waits=[(sPE, sPE.n)], inc=sSC)
                    gate_ev[gslot] = sSC.n
                    self.v(lambda e, p=p, ts=ts, ubank=ubank:
                           e.tensor_mul(
                               hid[:, 1024 * p + 512 * ts:
                                   1024 * p + 512 * ts + 512],
                               hid[:, 1024 * p + 512 * ts:
                                   1024 * p + 512 * ts + 512],
                               ubank[:, :]),
                           waits=[(sSC, sSC.n)], inc=sVE)
                    up_ev[gslot] = sVE.n
                if s_idx + 3 < len(sched):
                    issue_gu(s_idx + 3)
                if th == 0 and dn_pending:
                    hc2 = dn_pending.pop(0)
                    self.g(lambda e, ei=ei, hc2=hc2: e.dma_start(
                        dn[:, 2048 * hc2:2048 * (hc2 + 1)].rearrange(
                            'p (hc dd) -> p hc dd', hc=2),
                        self.DN_d[ei, 2 * hc2:2 * (hc2 + 1), :, :].rearrange(
                            'hc p dd -> p hc dd')),
                        waits=[dn_wait] if hc2 == 0 else [], inc=sDN)
                    if not dn_pending:
                        dn_loaded = sDN.n
            hid_done = sVE.n
            nxt_xe = None
            if u + 1 < len(units):
                nxt_xe = emit_xe(u + 1)
            di = 0
            for dc in range(NKC):
                for ts in range(2):
                    bank = pb[4 + di % 4]
                    for hc in range(16):
                        w = []
                        if hc == 0:
                            w = [(sVE, hid_done), (sDN, dn_loaded),
                                 (sVE, acc_ev[di % 4])]
                            if di % 4 >= 2 and nxt_xe is not None:
                                w.append((sVE, nxt_xe))
                        self.t(lambda e, dc=dc, hc=hc, ts=ts, bank=bank:
                               e.matmul(bank[:, :],
                                        dn[:, 1024 * hc + 128 * dc:
                                           1024 * hc + 128 * (dc + 1)],
                                        hid[:, 1024 * hc + 512 * ts:
                                            1024 * hc + 512 * ts + 512],
                                        start=(hc == 0),
                                        stop=(hc == 15)),
                               waits=w, inc=(sPE if hc == 15 else None))
                    col = dc * T_OWN + th0 + 512 * ts
                    self.v(lambda e, col=col, bank=bank: e.tensor_add(
                        self.resid[:, col:col + 512],
                        self.resid[:, col:col + 512], bank[:, :]),
                        waits=[(sPE, sPE.n)], inc=sVE)
                    acc_ev[di % 4] = sVE.n
                    di += 1
            xe_done = nxt_xe if nxt_xe is not None else xe_done
        self.moe_done = self.sVE.n
        self.dump('acc6', self.resid[:, :], [128, NKC * T_OWN], F32)

    def ph7(self, sb):
        sX, sPE, sVE = self.sX, self.sPE, self.sVE
        pb = self.pb
        ot = [self.at(f'ot{i}', [128, 2 * D], F32, self.D0 + 8192 * i)
              for i in range(2)]
        evac_cnt = {}
        store_cnt = [0, 0]
        for tc in range(16):
            obuf = ot[(tc // 2) % 2]
            ocol = (tc % 2) * D
            for kc in range(NKC):
                bank = pb[kc % 4]
                w = []
                if tc == 0 and kc == 0:
                    w.append((sVE, self.moe_done))
                if kc % 4 in evac_cnt:
                    w.append((sVE, evac_cnt[kc % 4]))
                self.t(lambda e, tc=tc, kc=kc, bank=bank: e.transpose(
                    bank[:, :128],
                    self.resid[:, kc * T_OWN + 128 * tc:
                               kc * T_OWN + 128 * (tc + 1)],
                    self.idf[:, :]),
                    waits=w, inc=sPE)
                self.v(lambda e, obuf=obuf, ocol=ocol, kc=kc, bank=bank:
                       e.tensor_copy(obuf[:, ocol + 128 * kc:
                                          ocol + 128 * (kc + 1)],
                                     bank[:, :128]),
                       waits=[(sPE, sPE.n)] +
                             ([(sX, store_cnt[(tc // 2) % 2])]
                              if kc == 0 else []),
                       inc=sVE)
                evac_cnt[kc % 4] = sVE.n
            if tc % 2 == 1:
                self.vsync()
                self.y(lambda e, tc=tc, obuf=obuf: e.dma_start(
                    self.out_d[128 * (tc - 1):128 * (tc + 1), :].rearrange(
                        '(c p) f -> p c f', p=128),
                    obuf[:, :].rearrange('p (c f) -> p c f', c=2)),
                    waits=[(sVE, sVE.n)], inc=self.sST)
                store_cnt[(tc // 2) % 2] = self.sST.n
        self.wait('sync', self.sST)


def host_prep(inputs):
    gu = inputs['L0_ffn_gate_up_w']
    dn = inputs['L0_ffn_down_w']
    WG = np.empty((NE, 16, 128, 2048), np.float32)
    for e in range(NE):
        T = np.ascontiguousarray(gu[e].T)
        gate = T[:, :2048].reshape(1024, 16, 128)
        up = T[:, 2048:].reshape(1024, 16, 128)
        paired = np.concatenate([gate, up], axis=2)
        arr = paired.reshape(8, 128, 16, 256).transpose(2, 1, 0, 3)
        WG[e] = arr.reshape(16, 128, 2048)
    DN = np.empty((NE, 16, 128, D), np.float32)
    for e in range(NE):
        T = np.ascontiguousarray(dn[e].T)
        DN[e] = T.reshape(16, 128, D)
    wkT16 = np.ascontiguousarray(
        inputs['L0_attn_Wk'].T.reshape(H, HS, D)).astype(np.float32)
    rT = inputs['L0_ffn_router_w'].T.astype(np.float32)
    routerT = np.ascontiguousarray(
        rT.reshape(8, 128, 8).transpose(1, 0, 2).reshape(128, 64))
    shared = {
        'wq': np.ascontiguousarray(inputs['L0_attn_Wq']),
        'wk': np.ascontiguousarray(inputs['L0_attn_Wk']),
        'wv': np.ascontiguousarray(inputs['L0_attn_Wv']),
        'wo': np.ascontiguousarray(inputs['L0_attn_Wo']),
        'bo': np.ascontiguousarray(inputs['L0_attn_bo']),
        'wkT16': wkT16,
        'routerT': routerT,
        'gamma': np.ascontiguousarray(inputs['L0_ffn_ln_gamma']),
        'beta': np.ascontiguousarray(inputs['L0_ffn_ln_beta']),
        'WG': WG, 'DN': DN,
    }
    x = inputs['L0_attn_x']
    in_maps = []
    for c in range(8):
        b, hh = c // 2, c % 2
        im = dict(shared)
        im['x_own'] = np.ascontiguousarray(x[b, hh * 2048:(hh + 1) * 2048])
        im['x_part'] = np.ascontiguousarray(
            x[b, (1 - hh) * 2048:(2 - hh) * 2048])
        in_maps.append(im)
    return in_maps


def kernel(**inputs):
    in_maps = host_prep(inputs)
    kb = KB(upto=7)
    nc = kb.build()
    from concourse.bass_utils import run_bass_kernel_spmd
    res = run_bass_kernel_spmd(nc, in_maps, core_ids=list(range(8)))
    out = np.empty((4, 4096, 1024), np.float32)
    for c in range(8):
        b, hh = c // 2, c % 2
        out[b, hh * 2048:(hh + 1) * 2048] = np.asarray(
            res.results[c]['out'], np.float32)
    return out
